# version 01
# baseline (speedup 1.0000x reference)
import numpy as np
from contextlib import ExitStack

import concourse.bass as bass
import concourse.mybir as mybir
from concourse.bass_utils import run_bass_kernel_spmd

F32 = mybir.dt.float32
BF16 = mybir.dt.bfloat16
AF = mybir.ActivationFunctionType
ALU = mybir.AluOpType
AX = mybir.AxisListType

NCORES = 8
T = 1024
HALO = 512
D = 4096
DC = D // 128
NTC = T // 128
NEG = -30000.0

ENGINES = ("sync", "tensor", "vector", "scalar", "gpsimd")


class Sem:
    __slots__ = ("name", "val", "h")

    def __init__(self, name):
        self.name, self.val, self.h = name, 0, None


class Prog:

    def __init__(self):
        self.q = {e: [] for e in ENGINES}
        self.sems = []
        self.future_waits = []

    def sem(self, name):
        s = Sem(name)
        self.sems.append(s)
        return s

    def wait(self, eng, sem, val, future=False):
        if val > sem.val and not future:
            raise RuntimeError(f"wait {eng} {sem.name}>={val} but planned max {sem.val}")
        if future:
            self.future_waits.append((sem, val))
        self.q[eng].append(lambda e, s=sem, v=val: e.wait_ge(s.h, v))

    def op(self, eng, fn, inc=None, inc_n=1):
        if inc is not None:
            inc.val += inc_n

        def run(e, fn=fn, inc=inc, n=inc_n):
            ins = fn(e)
            if inc is not None:
                ins.then_inc(inc.h, n)

        self.q[eng].append(run)
        return inc.val if inc is not None else None

    def dma(self, eng, out, in_, inc):
        return self.op(eng, lambda e, o=out, i=in_: e.dma_start(out=o, in_=i), inc, 16)


class G:

    def __init__(self, name):
        self.nc = bass.Bass("TRN2", target_bir_lowering=False)
        self.name = name
        self.P = Prog()
        self.es = ExitStack()
        self.inputs = {}
        self.outputs = {}
        p = self.P
        self.ld = p.sem("ld")
        self.wld = p.sem("wld")
        self.st = p.sem("st")
        self.te = p.sem("te")
        self.ve = p.sem("ve")
        self.se = p.sem("se")
        self.ge = p.sem("ge")
        self.cc = p.sem("cc")

    def in_(self, name, shape, dt=F32):
        self.inputs[name] = (tuple(shape), dt)
        return self.nc.dram_tensor(name, list(shape), dt, kind="ExternalInput").ap()

    def out(self, name, shape, dt=F32):
        self.outputs[name] = (tuple(shape), dt)
        return self.nc.dram_tensor(name, list(shape), dt, kind="ExternalOutput").ap()

    def dram(self, name, shape, dt=F32):
        return self.nc.dram_tensor(name, list(shape), dt, kind="Internal").ap()

    def _uname(self, name):
        n = getattr(self, "_ncnt", 0)
        self._ncnt = n + 1
        return f"{name}_{n}"

    def sb(self, name, shape, dt, es=None):
        return (es or self.es).enter_context(
            self.nc.sbuf_tensor(self._uname(name), list(shape), dt))

    def ps(self, name, shape, dt=F32, es=None):
        return (es or self.es).enter_context(
            self.nc.psum_tensor(self._uname(name), list(shape), dt))

    def finalize(self):
        for sem, val in self.P.future_waits:
            if val > sem.val:
                raise RuntimeError(f"future wait {sem.name}>={val} never reached (max {sem.val})")
        nc = self.nc
        with self.es:
            for s in self.P.sems:
                s.h = self.es.enter_context(nc.semaphore(s.name))
            with nc.Block() as block:
                for eng in ENGINES:
                    items = self.P.q[eng]
                    if not items:
                        continue

                    def body(e, items=items):
                        for fn in items:
                            fn(e)
                    getattr(block, eng)(body)
        return nc


def fmv(t, c, ncols):
    return t[:].rearrange("p (c t) -> p c t", t=ncols)[:, c, :]



def make_consts(g):
    P = g.P
    c = {}
    ident = g.in_("c_ident", [128, 128], F32)
    c["ident_f"] = g.sb("ident_f", [128, 128], F32)
    c["ident_b"] = g.sb("ident_b", [128, 128], BF16)
    c["ones_col_f"] = g.sb("ones_col_f", [128, 1], F32)
    c["ones_row_f"] = g.sb("ones_row_f", [1, 1024], F32)
    c["ones_t"] = g.sb("ones_t", [128, 128], F32)
    P.dma("sync", c["ident_f"][:], ident, g.ld)
    P.wait("vector", g.ld, g.ld.val)
    P.op("vector", lambda v: v.tensor_copy(c["ident_b"][:], c["ident_f"][:]))
    P.op("vector", lambda v: v.memset(c["ones_col_f"][:], 1.0))
    P.op("vector", lambda v: v.memset(c["ones_t"][:], 1.0))
    vev = P.op("vector", lambda v: v.memset(c["ones_row_f"][:], 1.0), g.ve)
    for eng in ("tensor", "scalar", "gpsimd"):
        P.wait(eng, g.ve, vev)
    return c



def emit_ln_stream(g, consts, src_dram, ntok, y_out, ps_a, ps_b, y_dram=None):
    P, ld, te, ve, se = g.P, g.ld, g.te, g.ve, g.se
    es = ExitStack()
    xs = [g.sb(f"lnx{i}", [128, DC * 128], F32, es) for i in range(2)]
    xq_s = g.sb("lnq", [128, DC * 128], F32, es)
    rows_s = g.sb("lnrow_s", [1, 128], F32, es)
    rows_q = g.sb("lnrow_q", [1, 128], F32, es)
    rows_t = g.sb("lnrow_t", [1, 128], F32, es)
    ab = g.sb("lnab", [128, 256], F32, es)
    nch = ntok // 128
    xv = src_dram.rearrange("(c p) t -> c p t", p=128)
    done = []
    rows_done = []
    ab_done = []
    apply_done = []
    for i in range(nch):
        buf, qbuf = xs[i % 2], xq_s
        if i >= 2:
            P.wait("sync", ve, apply_done[i - 2])
        P.dma("sync", buf[:].rearrange("p (c t) -> p c t", t=128),
              xv[:, :, i * 128:(i + 1) * 128].transpose([1, 0, 2]), ld)
        ldv = ld.val
        P.wait("scalar", ld, ldv)
        if i >= 1:
            P.wait("scalar", te, sq_te)
        sev = P.op("scalar", lambda s, b=buf, q=qbuf: s.activation(q[:], b[:], AF.Square), se)
        P.wait("tensor", ld, ldv)
        if i >= 1:
            P.wait("tensor", ve, rows_done[i - 1])
        for c in range(DC):
            P.op("tensor", lambda t, b=buf, c=c: t.matmul(
                ps_a[0:1, 0:128], consts["ones_col_f"][:],
                fmv(b, c, 128), start=(c == 0), stop=(c == DC - 1)))
        P.wait("tensor", se, sev)
        for c in range(DC):
            tev = P.op("tensor", lambda t, q=qbuf, c=c: t.matmul(
                ps_a[0:1, 128:256], consts["ones_col_f"][:],
                fmv(q, c, 128), start=(c == 0), stop=(c == DC - 1)),
                g.te if c == DC - 1 else None)
        sq_te = tev
        P.wait("vector", te, tev)
        P.op("vector", lambda v: v.tensor_copy(rows_s[:], ps_a[0:1, 0:128]))
        vev = P.op("vector", lambda v: v.tensor_copy(rows_q[:], ps_a[0:1, 128:256]), ve)
        rows_done.append(vev)
        P.op("vector", lambda v: v.tensor_scalar_mul(rows_s[:], rows_s[:], 1.0 / D))
        P.op("vector", lambda v: v.tensor_scalar_mul(rows_q[:], rows_q[:], 1.0 / D))
        P.op("vector", lambda v: v.tensor_mul(rows_t[:], rows_s[:], rows_s[:]))
        P.op("vector", lambda v: v.tensor_sub(rows_q[:], rows_q[:], rows_t[:]))
        vev = P.op("vector", lambda v: v.tensor_scalar_add(rows_q[:], rows_q[:], 1e-5), ve)
        P.wait("scalar", ve, vev)
        P.op("scalar", lambda s: s.activation(rows_q[:], rows_q[:], AF.Ln))
        sev = P.op("scalar", lambda s: s.activation(rows_q[:], rows_q[:], AF.Exp, scale=-0.5), se)
        P.wait("vector", se, sev)
        vev = P.op("vector", lambda v: v.tensor_scalar_mul(rows_t[:], rows_s[:], -1.0), ve)
        P.wait("tensor", ve, vev)
        if i >= 1:
            P.wait("tensor", ve, ab_done[i - 1])
        P.op("tensor", lambda t: t.matmul(
            ps_b[:, 0:128], consts["ones_row_f"][0:1, 0:128], rows_q[:], start=True, stop=True))
        tev = P.op("tensor", lambda t: t.matmul(
            ps_b[:, 128:256], consts["ones_row_f"][0:1, 0:128], rows_t[:], start=True, stop=True), te)
        P.wait("vector", te, tev)
        vev = P.op("vector", lambda v: v.tensor_copy(ab[:], ps_b[:, 0:256]), ve)
        ab_done.append(vev)
        a_view = ab[:, 0:128].unsqueeze(1).broadcast_to([128, DC, 128])
        b_view = ab[:, 128:256].unsqueeze(1).broadcast_to([128, DC, 128])
        bufv = buf[:].rearrange("p (c t) -> p c t", t=128)
        if y_dram is None:
            yv = y_out[:].rearrange("p (c t) -> p c t", t=ntok)[:, :, i * 128:(i + 1) * 128]
        else:
            stgb = y_out[i % 2]
            if i >= 2:
                P.wait("vector", g.st, done[i - 2])
            yv = stgb[:].rearrange("p (c t) -> p c t", t=128)
        P.op("vector", lambda v, yv=yv, bufv=bufv, b_view=b_view: v.tensor_tensor(
            yv, bufv, b_view, ALU.add))
        vev = P.op("vector", lambda v, yv=yv, a_view=a_view: v.tensor_tensor(
            yv, yv, a_view, ALU.mult), ve)
        apply_done.append(vev)
        if y_dram is None:
            done.append(vev)
        else:
            P.wait("sync", ve, vev)
            ydv = y_dram.rearrange("(c p) t -> c p t", p=128)
            P.dma("sync", ydv[:, :, i * 128:(i + 1) * 128].transpose([1, 0, 2]),
                  stgb[:].rearrange("p (c t) -> p c t", t=128), g.st)
            done.append(g.st.val)
    es.close()
    return done



class WStream:
    def __init__(self, g, name, wT, din, dout, blk=2, es=None):
        self.g, self.wT, self.din, self.dout, self.blk = g, wT, din, dout, blk
        self.kc = din // 128
        self.ndc = (dout + 127) // 128
        self.nblk = (self.ndc + blk - 1) // blk
        self.bufs = [g.sb(f"ws_{name}{i}", [128, self.kc * blk * 128], BF16, es)
                     for i in range(2)]
        self.loaded = []

    def issue_loads(self, consumed_sem, consumed_vals):
        g, P = self.g, self.g.P
        for b in range(self.nblk):
            if b >= 2:
                P.wait("gpsimd", consumed_sem, consumed_vals[b - 2], future=True)
            c0 = b * self.blk * 128
            c1 = min(self.dout, c0 + self.blk * 128)
            cols = self.wT[:, c0:c1]
            dst = self.bufs[b % 2][:].rearrange("p (k c) -> p k c", k=self.kc)[:, :, 0:c1 - c0]
            P.dma("gpsimd", dst, cols.rearrange("(k p) c -> p k c", p=128), g.wld)
            self.loaded.append(g.wld.val)

    def tile(self, dc, k):
        b, dcl = divmod(dc, self.blk)
        w = min(self.dout - dc * 128, 128)
        v = self.bufs[b % 2][:].rearrange("p (k c) -> p k c", k=self.kc)
        return v[:, k, dcl * 128:dcl * 128 + w], self.loaded[b]



def stage_l0_ffn(g, consts, xsrc, xdst, dbg=None):
    P = g.P
    ld, wld, st, te, ve, se = g.ld, g.wld, g.st, g.te, g.ve, g.se
    wbT = g.in_("l0f_wbT", [D, D], F32)
    wrT = g.in_("l0f_wrT", [D, 8], F32)
    acatT = g.in_("l0f_acatT", [D, 128], F32)
    bcatT = g.in_("l0f_bcatT", [128, D], F32)
    bbfm = g.in_("l0f_bbfm", [128, DC], F32)
    e16 = g.in_("l0f_e16", [8, 128], F32)

    es = ExitStack()
    y = g.sb("ffn_y", [128, DC * T], BF16, es)
    wr_s = g.sb("ffn_wr", [128, DC * 8], BF16, es)
    acat = g.sb("ffn_acat", [128, DC * 128], BF16, es)
    bcat = g.sb("ffn_bcat", [128, D], BF16, es)
    bb_s = g.sb("ffn_bb", [128, DC], F32, es)
    e16_s = g.sb("ffn_e16", [8, 128], F32, es)
    lg_tm = g.sb("ffn_lgtm", [128, NTC * 8], F32, es)
    lgps = g.sb("ffn_lgf", [8, T], F32, es)
    cmb_fm = g.sb("ffn_cmbfm", [8, T], F32, es)
    cmbx = g.sb("ffn_cmbx", [128, T], F32, es)
    hw = g.sb("ffn_hw", [128, T], BF16, es)
    xres = [g.sb(f"ffn_xres{i}", [128, T], F32, es) for i in range(2)]
    xout = [g.sb(f"ffn_xout{i}", [128, T], F32, es) for i in range(2)]
    sc = g.sb("ffn_sc", [128, NTC * 8], F32, es)
    m1 = g.sb("ffn_m1", [128, NTC], F32, es)
    m2 = g.sb("ffn_m2", [128, NTC], F32, es)
    sw = g.sb("ffn_sw", [128, NTC], F32, es)
    cmb_tm = g.sb("ffn_cmbtm", [128, NTC * 8], F32, es)
    ps0 = g.ps("ffn_ps0", [128, 512], F32, es)
    ps1 = g.ps("ffn_ps1", [128, 512], F32, es)
    ps2 = g.ps("ffn_ps2", [128, 512], F32, es)
    ps3 = g.ps("ffn_ps3", [128, 512], F32, es)
    ps_r = g.ps("ffn_psr", [8, 1024], F32, es)
    ps_t = g.ps("ffn_pst", [128, 256], F32, es)

    P.dma("gpsimd", wr_s[:].rearrange("p (k c) -> p k c", c=8),
          wrT.rearrange("(k p) c -> p k c", p=128), wld)
    P.dma("gpsimd", acat[:].rearrange("p (k c) -> p k c", c=128),
          acatT.rearrange("(k p) c -> p k c", p=128), wld)
    P.dma("gpsimd", bcat[:], bcatT, wld)
    P.dma("sync", bb_s[:], bbfm, ld)
    P.dma("sync", e16_s[:], e16, ld)
    small_wld = wld.val
    small_ld = ld.val

    ws = WStream(g, "wb", wbT, D, D, blk=1, es=es)

    ln_done = emit_ln_stream(g, consts, xsrc, T, y, ps_r[0:1, 0:256], ps_t)
    ln_all = ln_done[-1]

    P.wait("tensor", wld, small_wld)
    P.wait("tensor", ld, small_ld)
    P.wait("tensor", ve, ln_all)
    for half in range(2):
        for k in range(DC):
            tev = P.op("tensor", lambda t, k=k, half=half: t.matmul(
                ps_r[:, half * 512:(half + 1) * 512],
                wr_s[:].rearrange("p (k c) -> p k c", c=8)[:, k, :],
                fmv(y, k, T)[:, half * 512:(half + 1) * 512],
                start=(k == 0), stop=(k == DC - 1)),
                te if (k == DC - 1 and half == 1) else None)
    P.wait("vector", te, tev)
    vev = P.op("vector", lambda v: v.tensor_copy(lgps[:], ps_r[:]), ve)
    P.wait("tensor", ve, vev)
    for i in range(NTC):
        tev = P.op("tensor", lambda t, i=i: t.transpose(
            ps_t[:, i * 8:(i + 1) * 8], lgps[0:8, i * 128:(i + 1) * 128],
            consts["ident_f"][0:8, 0:8]), te if i == NTC - 1 else None)
    P.wait("vector", te, tev)
    vev = P.op("vector", lambda v: v.tensor_copy(lg_tm[:], ps_t[:, 0:NTC * 8]), ve)

    lgv = lg_tm[:].rearrange("p (i e) -> p i e", e=8)
    scv = sc[:].rearrange("p (i e) -> p i e", e=8)
    cmv = cmb_tm[:].rearrange("p (i e) -> p i e", e=8)
    def vstep(fn):
        vv = P.op("vector", fn, ve)
        P.wait("vector", ve, vv)
        return vv
    for i in range(NTC):
        li, si, ci = lgv[:, i, :], scv[:, i, :], cmv[:, i, :]
        m1i, m2i, swi = m1[:, i:i + 1], m2[:, i:i + 1], sw[:, i:i + 1]
        vstep(lambda v, li=li, m1i=m1i: v.reduce_max(m1i, li, AX.X, negate=True))
        vstep(lambda v, li=li, si=si, m1i=m1i: v.tensor_scalar(
            si, li, -1.0, m1i, ALU.mult, ALU.is_equal))
        vstep(lambda v, li=li, si=si: v.scalar_tensor_tensor(
            si, si, NEG, li, ALU.mult, ALU.add))
        vstep(lambda v, si=si, m2i=m2i: v.reduce_max(m2i, si, AX.X))
        vev = vstep(lambda v, li=li, si=si, m2i=m2i: v.tensor_scalar(
            si, li, m2i, None, ALU.is_ge))
        P.wait("scalar", ve, vev)
        sev = P.op("scalar", lambda s, li=li, ci=ci, m1i=m1i: s.activation(
            ci, li, AF.Exp, bias=m1i), se)
        P.wait("vector", se, sev)
        vstep(lambda v, ci=ci, si=si: v.tensor_mul(ci, ci, si))
        vstep(lambda v, ci=ci, swi=swi: v.reduce_sum(swi, ci, AX.X))
        vstep(lambda v, swi=swi: v.reciprocal(swi, swi))
        cmb_ready = vstep(lambda v, ci=ci, swi=swi: v.tensor_scalar_mul(ci, ci, swi))

    P.wait("tensor", ve, cmb_ready)
    for i in range(NTC):
        tev = P.op("tensor", lambda t, i=i: t.transpose(
            ps_r[0:8, i * 128:(i + 1) * 128], cmv[:, i, :],
            consts["ident_f"][:]), te if i == NTC - 1 else None)
    P.wait("vector", te, tev)
    vev = P.op("vector", lambda v: v.tensor_copy(cmb_fm[:], ps_r[:]), ve)

    P.wait("tensor", ve, vev)
    for half in range(2):
        tev = P.op("tensor", lambda t, half=half: t.matmul(
            (ps0, ps1)[half][:], e16_s[:],
            cmb_fm[:, half * 512:(half + 1) * 512], start=True, stop=True),
            te if half == 1 else None)
    P.wait("vector", te, tev)
    P.op("vector", lambda v: v.tensor_copy(cmbx[:, 0:512], ps0[:]))
    vev = P.op("vector", lambda v: v.tensor_copy(cmbx[:, 512:1024], ps1[:]), ve)

    P.wait("tensor", ve, vev)
    for half in range(2):
        for k in range(DC):
            tev = P.op("tensor", lambda t, k=k, half=half: t.matmul(
                (ps2, ps3)[half][:], fmv(acat, k, 128),
                fmv(y, k, T)[:, half * 512:(half + 1) * 512],
                start=(k == 0), stop=(k == DC - 1)),
                te if (k == DC - 1 and half == 1) else None)
    P.wait("vector", te, tev)
    P.op("vector", lambda v: v.tensor_mul(hw[:, 0:512], ps2[:], cmbx[:, 0:512]))
    vev = P.op("vector", lambda v: v.tensor_mul(hw[:, 512:1024], ps3[:], cmbx[:, 512:1024]), ve)
    hw_ready = vev

    if dbg is not None:
        for nm, tens, shape in [("lg", lgps, [8, T]), ("cmb", cmb_fm, [8, T]),
                                ("lgtm", lg_tm, [128, NTC * 8]), ("sc", sc, [128, NTC * 8]),
                                ("m1", m1, [128, NTC]), ("m2", m2, [128, NTC]),
                                ("sw", sw, [128, NTC]), ("cmbtm", cmb_tm, [128, NTC * 8]),
                                ("cmbx", cmbx, [128, T]), ("hw", hw, [128, T]),
                                ("yln", y, [128, DC * T])]:
            if nm in dbg:
                od = g.out("dbg_" + nm, shape, tens[:].dtype)
                P.wait("sync", ve, hw_ready)
                P.dma("sync", od, tens[:], st)

    te_start = te.val
    consumed = [te_start + (b + 1) * ws.blk for b in range(ws.nblk)]
    ws.issue_loads(te, consumed)

    xv = xsrc.rearrange("(c p) t -> c p t", p=128)
    dv = xdst.rearrange("(c p) t -> c p t", p=128)
    ev_dc = {}
    st_dc = {}
    for dc in range(DC):
        pb = (ps0, ps1) if dc % 2 == 0 else (ps2, ps3)
        if dc >= 2:
            P.wait("sync", ve, ev_dc[dc - 2][0])
        P.dma("sync", xres[dc % 2][:], xv[dc], ld)
        res_ld = ld.val
        b = dc // ws.blk
        P.wait("tensor", wld, ws.loaded[b])
        if dc >= 2:
            P.wait("tensor", ve, ev_dc[dc - 2][1])
        if dc == 0:
            P.wait("tensor", ve, hw_ready)
        for half in range(2):
            P.op("tensor", lambda t, dc=dc, half=half, pb=pb: t.matmul(
                pb[half][:], bcat[:, dc * 128:(dc + 1) * 128],
                hw[:, half * 512:(half + 1) * 512], start=True, stop=False))
        for k in range(DC):
            tile, lv = ws.tile(dc, k)
            for half in range(2):
                last = (k == DC - 1)
                P.op("tensor", lambda t, tile=tile, half=half, last=last, pb=pb, k=k: t.matmul(
                    pb[half][:], tile,
                    fmv(y, k, T)[:, half * 512:(half + 1) * 512],
                    start=False, stop=last), te if (last and half == 1) else None)
        te_dc = te.val
        P.wait("vector", ld, res_ld)
        P.wait("vector", te, te_dc)
        if dc >= 2:
            P.wait("vector", st, st_dc[dc - 2])
        e0 = P.op("vector", lambda v, dc=dc, pb=pb: v.scalar_tensor_tensor(
            xout[dc % 2][:, 0:512], pb[0][:], bb_s[:, dc:dc + 1],
            xres[dc % 2][:, 0:512], ALU.add, ALU.add), ve)
        e1 = P.op("vector", lambda v, dc=dc, pb=pb: v.scalar_tensor_tensor(
            xout[dc % 2][:, 512:1024], pb[1][:], bb_s[:, dc:dc + 1],
            xres[dc % 2][:, 512:1024], ALU.add, ALU.add), ve)
        ev_dc[dc] = (e0, e1)
        P.wait("sync", ve, e1)
        P.dma("sync", dv[dc], xout[dc % 2][:], st)
        st_dc[dc] = st.val
    es.close()
    return st.val


def finish(g):
    g.P.wait("sync", g.st, g.st.val)



def _ct(a):
    return np.ascontiguousarray(a)


def host_shared_consts():
    return {"c_ident": np.eye(128, dtype=np.float32)}


def host_l0ffn_weights(inp):
    e16 = np.zeros((8, 128), dtype=np.float32)
    for e in range(8):
        e16[e, e * 16:(e + 1) * 16] = 2.0
    return {
        "l0f_wbT": _ct(np.asarray(inp["L0_ffn_W_base"]).T),
        "l0f_wrT": _ct(np.asarray(inp["L0_ffn_W_router"]).T),
        "l0f_acatT": _ct(np.asarray(inp["L0_ffn_A"]).reshape(128, D).T),
        "l0f_bcatT": _ct(np.asarray(inp["L0_ffn_B"]).transpose(0, 2, 1).reshape(128, D)),
        "l0f_bbfm": _ct(np.asarray(inp["L0_ffn_b_base"]).reshape(DC, 128).T),
        "l0f_e16": e16,
    }


def shard_stream(x):
    out = []
    for core in range(NCORES):
        b, j = divmod(core, 4)
        sl = x[b, j * T:(j + 1) * T, :]
        out.append(_ct(np.asarray(sl).T.astype(np.float32)))
    return out


def unshard_stream(parts):
    x = np.zeros((2, 4 * T, D), dtype=np.float32)
    for core in range(NCORES):
        b, j = divmod(core, 4)
        x[b, j * T:(j + 1) * T, :] = parts[core].T
    return x


def run_launch(g, percore_inputs, shared_inputs):
    nc = g.finalize()
    in_maps = []
    for core in range(NCORES):
        m = {}
        for name, (shape, dt) in g.inputs.items():
            if name in percore_inputs[core]:
                m[name] = percore_inputs[core][name]
            elif name in shared_inputs:
                m[name] = shared_inputs[name]
            else:
                raise KeyError(f"missing input {name}")
            assert tuple(m[name].shape) == shape, (name, m[name].shape, shape)
        in_maps.append(m)
    res = run_bass_kernel_spmd(nc, in_maps, core_ids=list(range(NCORES)))
    return res.results



def stage_l1_ffn(g, consts, xsrc, xdst):
    P = g.P
    ld, wld, st, te, ve, se = g.ld, g.wld, g.st, g.te, g.ve, g.se
    wT = g.in_("l1f_wT", [D, D], F32)
    h128 = g.in_("l1f_h128", [128, 128], F32)
    loraA = g.in_("l1f_loraA", [D, 16], F32)
    loraB = g.in_("l1f_loraB", [16, D], F32)
    bbfm = g.in_("l1f_bfm", [128, DC], F32)

    es = ExitStack()
    y = g.sb("hff_y", [128, DC * T], BF16, es)
    h128_b = g.sb("hff_h", [128, 128], BF16, es)
    lA = g.sb("hff_lA", [128, DC * 16], BF16, es)
    lB = g.sb("hff_lB", [16, D], BF16, es)
    bb_s = g.sb("hff_bb", [128, DC], F32, es)
    p1tm = g.sb("hff_p1tm", [128, NTC * 16], BF16, es)
    p1fm = g.sb("hff_p1fm", [16, T], BF16, es)
    xres = [g.sb(f"hff_xres{i}", [128, T], F32, es) for i in range(2)]
    xout = [g.sb(f"hff_xout{i}", [128, T], F32, es) for i in range(2)]
    ps0 = g.ps("hff_ps0", [128, 512], F32, es)
    ps1 = g.ps("hff_ps1", [128, 512], F32, es)
    ps2 = g.ps("hff_ps2", [128, 512], F32, es)
    ps3 = g.ps("hff_ps3", [128, 512], F32, es)
    ps_a = g.ps("hff_psa", [1, 256], F32, es)
    ps_b = g.ps("hff_psb", [128, 256], F32, es)
    ps_l = g.ps("hff_psl", [128, 16], F32, es)
    ps_lt = g.ps("hff_pslt", [16, 128], BF16, es)

    P.dma("gpsimd", h128_b[:], h128, wld)
    P.dma("gpsimd", lA[:].rearrange("p (k r) -> p k r", r=16),
          loraA.rearrange("(k p) r -> p k r", p=128), wld)
    P.dma("gpsimd", lB[:], loraB, wld)
    small_wld = wld.val
    P.dma("sync", bb_s[:], bbfm, ld)
    small_ld = ld.val

    ws = WStream(g, "w1f", wT, D, D, blk=1, es=es)

    ln_done = emit_ln_stream(g, consts, xsrc, T, y, ps_a, ps_b)
    ln_all = ln_done[-1]
    z = g.sb("hff_z", [128, DC * T], BF16, es)

    P.wait("tensor", wld, small_wld)
    P.wait("tensor", ld, small_ld)
    lav = lA[:].rearrange("p (k r) -> p k r", r=16)
    for i in range(NTC):
        P.wait("tensor", ve, ln_done[i])
        for k in range(DC):
            tev = P.op("tensor", lambda t, i=i, k=k: t.matmul(
                ps_l[:], fmv(y, k, T)[:, i * 128:(i + 1) * 128], lav[:, k, :],
                start=(k == 0), stop=(k == DC - 1)), te if k == DC - 1 else None)
        P.wait("scalar", te, tev)
        sev = P.op("scalar", lambda s, i=i: s.activation(
            p1tm[:, i * 16:(i + 1) * 16], ps_l[:], AF.Copy), se)
        P.wait("tensor", se, sev)
    for i in range(NTC):
        tev = P.op("tensor", lambda t, i=i: t.transpose(
            ps_lt[:], p1tm[:, i * 16:(i + 1) * 16], consts["ident_b"][:]), te)
        P.wait("scalar", te, tev)
        sev = P.op("scalar", lambda s, i=i: s.activation(
            p1fm[0:16, i * 128:(i + 1) * 128], ps_lt[:], AF.Copy, scale=0.01), se)
        if i < NTC - 1:
            P.wait("tensor", se, sev)
    p1_ready = sev

    for c in range(DC):
        pb = (ps0, ps1) if c % 2 == 0 else (ps2, ps3)
        if c >= 2:
            P.wait("tensor", ve, zev[c - 2])
        for half in range(2):
            P.op("tensor", lambda t, c=c, half=half, pb=pb: t.matmul(
                pb[half][:], h128_b[:],
                fmv(y, c, T)[:, half * 512:(half + 1) * 512], start=True, stop=True),
                te if half == 1 else None)
        tev = te.val
        P.wait("vector", te, tev)
        if c == 0:
            zev = {}
        P.op("vector", lambda v, c=c, pb=pb: v.tensor_copy(
            fmv(z, c, T)[:, 0:512], pb[0][:]))
        zev[c] = P.op("vector", lambda v, c=c, pb=pb: v.tensor_copy(
            fmv(z, c, T)[:, 512:1024], pb[1][:]), ve)
    z_all = zev[DC - 1]

    te_afterA = te.val
    P.wait("vector", te, te_afterA)
    P.wait("vector", ve, z_all)
    src, dst = z, y
    for s in range(5):
        h = 1 << s
        na = DC // (2 * h)
        sv = src[:].rearrange("p (a b r t) -> p a b r t", b=2, r=h, t=T)
        dv = dst[:].rearrange("p (a b r t) -> p a b r t", b=2, r=h, t=T)
        P.op("vector", lambda v, sv=sv, dv=dv: v.tensor_add(
            dv[:, :, 0, :, :], sv[:, :, 0, :, :], sv[:, :, 1, :, :]))
        vev = P.op("vector", lambda v, sv=sv, dv=dv: v.tensor_sub(
            dv[:, :, 1, :, :], sv[:, :, 0, :, :], sv[:, :, 1, :, :]), ve)
        P.wait("vector", ve, vev)
        src, dst = dst, src
    rot = src
    rot_ready = vev

    te_start = te.val
    consumed = [te_start + (b + 1) * ws.blk for b in range(ws.nblk)]
    ws.issue_loads(te, consumed)

    xv = xsrc.rearrange("(c p) t -> c p t", p=128)
    dvv = xdst.rearrange("(c p) t -> c p t", p=128)
    ev_dc = {}
    st_dc = {}
    for dc in range(DC):
        pb = (ps0, ps1) if dc % 2 == 0 else (ps2, ps3)
        if dc >= 2:
            P.wait("sync", ve, ev_dc[dc - 2][0])
        P.dma("sync", xres[dc % 2][:], xv[dc], ld)
        res_ld = ld.val
        b = dc // ws.blk
        P.wait("tensor", wld, ws.loaded[b])
        if dc >= 2:
            P.wait("tensor", ve, ev_dc[dc - 2][1])
        if dc == 0:
            P.wait("tensor", ve, rot_ready)
            P.wait("tensor", se, p1_ready)
        for half in range(2):
            P.op("tensor", lambda t, dc=dc, half=half, pb=pb: t.matmul(
                pb[half][:], lB[:, dc * 128:(dc + 1) * 128],
                p1fm[0:16, half * 512:(half + 1) * 512], start=True, stop=False))
        for k in range(DC):
            tile, lv = ws.tile(dc, k)
            for half in range(2):
                last = (k == DC - 1)
                P.op("tensor", lambda t, tile=tile, half=half, last=last, pb=pb, k=k: t.matmul(
                    pb[half][:], tile,
                    fmv(rot, k, T)[:, half * 512:(half + 1) * 512],
                    start=False, stop=last), te if (last and half == 1) else None)
        te_dc = te.val
        P.wait("vector", ld, res_ld)
        P.wait("vector", te, te_dc)
        if dc >= 2:
            P.wait("vector", st, st_dc[dc - 2])
        e0 = P.op("vector", lambda v, dc=dc, pb=pb: v.scalar_tensor_tensor(
            xout[dc % 2][:, 0:512], pb[0][:], bb_s[:, dc:dc + 1],
            xres[dc % 2][:, 0:512], ALU.add, ALU.add), ve)
        e1 = P.op("vector", lambda v, dc=dc, pb=pb: v.scalar_tensor_tensor(
            xout[dc % 2][:, 512:1024], pb[1][:], bb_s[:, dc:dc + 1],
            xres[dc % 2][:, 512:1024], ALU.add, ALU.add), ve)
        ev_dc[dc] = (e0, e1)
        P.wait("sync", ve, e1)
        P.dma("sync", dvv[dc], xout[dc % 2][:], st)
        st_dc[dc] = st.val
    es.close()
    return st.val


def host_l1ffn_weights(inp):
    i = np.arange(128)
    pc = np.bitwise_count(i[:, None] & i[None, :]).astype(np.int64)
    h128 = ((-1.0) ** pc).astype(np.float32) / 64.0
    return {
        "l1f_wT": _ct(np.asarray(inp["L1_ffn_W"]).T),
        "l1f_h128": h128,
        "l1f_loraA": _ct(np.asarray(inp["L1_ffn_lora_A"])),
        "l1f_loraB": _ct(np.asarray(inp["L1_ffn_lora_B"])),
        "l1f_bfm": _ct(np.asarray(inp["L1_ffn_b"]).reshape(DC, 128).T),
    }



TH = T + HALO
NHC = TH // 128
SCALE = 1.0 / np.sqrt(128.0)


def stage_l0_attn(g, consts, xsrc, xdst, dbg=None):
    P = g.P
    ld, wld, st, te, ve, se = g.ld, g.wld, g.st, g.te, g.ve, g.se
    wqT = g.in_("l0a_wqT", [D, D], F32)
    wkT = g.in_("l0a_wkT", [D, 1024], F32)
    wvT = g.in_("l0a_wvT", [D, 1024], F32)
    woT = g.in_("l0a_woT", [D, D], F32)
    maskS = g.in_("l0a_maskS", [NTC, 5, 128, 128], F32)

    y_dram = g.dram("l0a_y", [D, TH], BF16)

    es = ExitStack()
    masks = g.sb("att_masks", [128, NTC * 5 * 128], BF16, es)
    o_fm = g.sb("att_o", [128, DC * T], BF16, es)

    P.dma("gpsimd", masks[:].rearrange("p (q k j) -> p q k j", k=5, j=128),
          maskS.rearrange("q k i j -> i q k j"), wld)
    mask_wld = wld.val

    es_ln = ExitStack()
    ystg = [g.sb(f"att_ystg{i}", [128, DC * 128], BF16, es_ln) for i in range(2)]
    ps_a = g.ps("att_psa", [1, 256], F32, es_ln)
    ps_b = g.ps("att_psb", [128, 256], F32, es_ln)
    ln_done = emit_ln_stream(g, consts, xsrc, TH, ystg, ps_a, ps_b, y_dram=y_dram)
    y_ready = ln_done[-1]
    es_ln.close()

    es_grp = ExitStack()
    wqg = g.sb("att_wqg", [128, DC * 512], BF16, es_grp)
    wkg = g.sb("att_wkg", [128, DC * 128], BF16, es_grp)
    wvg = g.sb("att_wvg", [128, DC * 128], BF16, es_grp)
    ybuf = [g.sb(f"att_yb{i}", [128, 4 * TH], BF16, es_grp) for i in range(2)]
    kg = g.sb("att_kg", [128, TH], BF16, es_grp)
    vstage = g.sb("att_vs", [128, TH], BF16, es_grp)
    vtm = g.sb("att_vtm", [128, NHC * 128], BF16, es_grp)
    qg = g.sb("att_qg", [128, 4 * T], BF16, es_grp)
    ptm = [g.sb(f"att_ptm{i}", [128, 640], BF16, es_grp) for i in range(2)]
    pts = [g.sb(f"att_pts{i}", [128, 640], BF16, es_grp) for i in range(2)]
    mrow = g.sb("att_mrow", [128, 2], F32, es_grp)
    lrow = g.sb("att_lrow", [128, 2], F32, es_grp)
    yv_dram = y_dram.rearrange("(k p) t -> k p t", p=128)

    P.wait("sync", st, y_ready)
    slab_te = {}
    for grp in range(8):
        if grp >= 1:
            P.wait("gpsimd", te, slab_te[("B", grp - 1, 7)], future=True)
        else:
            P.wait("gpsimd", st, y_ready)
        P.dma("gpsimd", wqg[:].rearrange("p (k c) -> p k c", c=512),
              wqT[:, grp * 512:(grp + 1) * 512].rearrange("(k p) c -> p k c", p=128), wld)
        P.dma("gpsimd", wkg[:].rearrange("p (k c) -> p k c", c=128),
              wkT[:, grp * 128:(grp + 1) * 128].rearrange("(k p) c -> p k c", p=128), wld)
        P.dma("gpsimd", wvg[:].rearrange("p (k c) -> p k c", c=128),
              wvT[:, grp * 128:(grp + 1) * 128].rearrange("(k p) c -> p k c", p=128), wld)
        w_wld = wld.val

        es_pa = ExitStack()
        ps_kv = g.ps("att_pskv", [128, TH], F32, es_pa)
        ps_v = g.ps("att_psv", [128, TH], F32, es_pa)
        ps_vt = [g.ps(f"att_psvt{i}", [128, 128], BF16, es_pa) for i in range(2)]
        first = True
        for s in range(8):
            if s >= 2:
                P.wait("sync", te, slab_te[("A", grp, s - 2)], future=True)
            elif grp >= 1:
                P.wait("sync", te, slab_te[("B", grp - 1, 6 + s)], future=True)
            P.dma("sync", ybuf[s % 2][:].rearrange("p (k t) -> p k t", k=4),
                  yv_dram[s * 4:(s + 1) * 4].transpose([1, 0, 2]), ld)
            ldv = ld.val
            if first:
                P.wait("tensor", wld, w_wld)
                first = False
            P.wait("tensor", ld, ldv)
            for kk in range(4):
                k = s * 4 + kk
                yb = ybuf[s % 2][:].rearrange("p (k t) -> p k t", k=4)[:, kk, :]
                for third in range(3):
                    P.op("tensor", lambda t, k=k, third=third, yb=yb: t.matmul(
                        ps_kv[:, third * 512:(third + 1) * 512],
                        fmv(wkg, k, 128),
                        yb[:, third * 512:(third + 1) * 512],
                        start=(k == 0), stop=(k == DC - 1)))
                for third in range(3):
                    tev = P.op("tensor", lambda t, k=k, third=third, yb=yb: t.matmul(
                        ps_v[:, third * 512:(third + 1) * 512],
                        fmv(wvg, k, 128),
                        yb[:, third * 512:(third + 1) * 512],
                        start=(k == 0), stop=(k == DC - 1)),
                        te if kk == 3 else None)
            slab_te[("A", grp, s)] = te.val
        kv_te = te.val
        P.wait("vector", te, kv_te)
        P.op("vector", lambda v: v.tensor_copy(kg[:], ps_kv[:]))
        vev = P.op("vector", lambda v: v.tensor_copy(vstage[:], ps_v[:]), ve)
        P.wait("tensor", ve, vev)
        vt_ev = {}
        for tc in range(NHC):
            if tc >= 2:
                P.wait("tensor", ve, vt_ev[tc - 2])
            tev = P.op("tensor", lambda t, tc=tc: t.transpose(
                ps_vt[tc % 2][:], vstage[:, tc * 128:(tc + 1) * 128],
                consts["ident_b"][:]), te)
            P.wait("vector", te, tev)
            vt_ev[tc] = P.op("vector", lambda v, tc=tc: v.tensor_copy(
                vtm[:, tc * 128:(tc + 1) * 128], ps_vt[tc % 2][:]), ve)
        vtm_ready = vt_ev[NHC - 1]
        P.wait("tensor", ve, vtm_ready)
        es_pa.close()

        es_pb = ExitStack()
        psq_all = [g.ps(f"att_psq{i}", [128, 512], F32, es_pb) for i in range(8)]
        for s in range(8):
            if s >= 2:
                P.wait("sync", te, slab_te[("B", grp, s - 2)], future=True)
            else:
                P.wait("sync", te, slab_te[("A", grp, 6 + s)])
            P.dma("sync", ybuf[s % 2][:, 0:4 * T].rearrange("p (k t) -> p k t", k=4),
                  yv_dram[s * 4:(s + 1) * 4, :, HALO:TH].transpose([1, 0, 2]), ld)
            ldv = ld.val
            P.wait("tensor", ld, ldv)
            for kk in range(4):
                k = s * 4 + kk
                yb = ybuf[s % 2][:, 0:4 * T].rearrange("p (k t) -> p k t", k=4)[:, kk, :]
                for hc in range(4):
                    for half in range(2):
                        tev = P.op("tensor", lambda t, k=k, hc=hc, half=half, yb=yb: t.matmul(
                            psq_all[hc * 2 + half][:],
                            wqg[:].rearrange("p (k c) -> p k c", c=512)[:, k, hc * 128:(hc + 1) * 128],
                            yb[:, half * 512:(half + 1) * 512],
                            start=(k == 0), stop=(k == DC - 1)),
                            te if (kk == 3 and hc == 3 and half == 1) else None)
            slab_te[("B", grp, s)] = te.val
        q_te = te.val
        P.wait("vector", te, q_te)
        for hc in range(4):
            for half in range(2):
                vev = P.op("vector", lambda v, hc=hc, half=half: v.tensor_copy(
                    qg[:].rearrange("p (h t) -> p h t", h=4)[:, hc, half * 512:(half + 1) * 512],
                    psq_all[hc * 2 + half][:]), ve)
        q_ready = vev
        P.wait("tensor", ve, q_ready)
        es_pb.close()

        es_at = ExitStack()
        ps_s2 = [g.ps(f"att_pss{i}", [128, 640], F32, es_at) for i in range(2)]
        ps_p2 = [g.ps(f"att_psp{i}", [128, 640], BF16, es_at) for i in range(2)]
        ps_o2 = [g.ps(f"att_pso{i}", [128, 128], F32, es_at) for i in range(2)]
        if grp == 0:
            P.wait("tensor", wld, mask_wld)
        pair_expse = {}
        pair_oev = {}
        pair_ptsve = {}
        pair_trte = {}
        pidx = 0
        for hc in range(4):
            for qt in range(NTC):
                pp = pidx % 2
                psS, psP, psO = ps_s2[pp], ps_p2[pp], ps_o2[pp]
                mcol = mrow[:, pp:pp + 1]
                lcol = lrow[:, pp:pp + 1]
                if pidx >= 2:
                    P.wait("tensor", se, pair_oev[pidx - 2], future=True)
                    P.wait("tensor", ve, pair_ptsve[pidx - 2], future=True)
                mview = masks[:].rearrange("p (q j) -> p q j", j=640)[:, qt, :]
                P.op("tensor", lambda t, mview=mview, psS=psS: t.matmul(
                    psS[:, 0:512], consts["ident_b"][:], mview[:, 0:512],
                    start=True, stop=False))
                P.op("tensor", lambda t, mview=mview, psS=psS: t.matmul(
                    psS[:, 512:640], consts["ident_b"][:], mview[:, 512:640],
                    start=True, stop=False))
                for kt in range(5):
                    lhs = qg[:].rearrange("p (h t) -> p h t", h=4)[:, hc, qt * 128:(qt + 1) * 128]
                    rhs = kg[:, (qt + kt) * 128:(qt + kt + 1) * 128]
                    stop = kt in (3, 4)
                    tev = P.op("tensor", lambda t, lhs=lhs, rhs=rhs, kt=kt, stop=stop, psS=psS: t.matmul(
                        psS[:, kt * 128:(kt + 1) * 128], lhs, rhs,
                        start=False, stop=stop), te if kt == 4 else None)
                P.wait("vector", te, tev)
                if pidx >= 2:
                    P.wait("vector", se, pair_expse[pidx - 2])
                vv = P.op("vector", lambda v, psS=psS, mcol=mcol: v.reduce_max(
                    mcol, psS[:], AX.X, negate=True), ve)
                P.wait("vector", ve, vv)
                vv = P.op("vector", lambda v, mcol=mcol: v.tensor_scalar_mul(
                    mcol, mcol, SCALE), ve)
                P.wait("scalar", ve, vv)
                if pidx >= 2:
                    P.wait("scalar", te, pair_trte[pidx - 2])
                sv = P.op("scalar", lambda s, psS=psS, mcol=mcol, lcol=lcol, pp=pp: s.activation(
                    ptm[pp][:], psS[:], AF.Exp, bias=mcol, scale=SCALE,
                    accum_out=lcol), se)
                pair_expse[pidx] = sv
                P.wait("vector", se, sv)
                vv = P.op("vector", lambda v, lcol=lcol: v.reciprocal(lcol, lcol), ve)
                P.wait("vector", ve, vv)
                vv = P.op("vector", lambda v, pp=pp, lcol=lcol: v.tensor_scalar_mul(
                    ptm[pp][:], ptm[pp][:], lcol), ve)
                P.wait("tensor", ve, vv)
                for kt in range(5):
                    tev = P.op("tensor", lambda t, pp=pp, kt=kt, psP=psP: t.transpose(
                        psP[:, kt * 128:(kt + 1) * 128],
                        ptm[pp][:, kt * 128:(kt + 1) * 128], consts["ident_b"][:]),
                        te if kt == 4 else None)
                pair_trte[pidx] = tev
                P.wait("vector", te, tev)
                vv = P.op("vector", lambda v, pp=pp, psP=psP: v.tensor_copy(
                    pts[pp][:], psP[:]), ve)
                pair_ptsve[pidx] = vv
                P.wait("tensor", ve, vv)
                for kt in range(5):
                    lhs = vtm[:, (qt + kt) * 128:(qt + kt + 1) * 128]
                    tev = P.op("tensor", lambda t, lhs=lhs, pp=pp, kt=kt, psO=psO: t.matmul(
                        psO[:], lhs, pts[pp][:, kt * 128:(kt + 1) * 128],
                        start=(kt == 0), stop=(kt == 4)), te if kt == 4 else None)
                P.wait("scalar", te, tev)
                oc = 4 * grp + hc
                sv = P.op("scalar", lambda s, oc=oc, qt=qt, psO=psO: s.activation(
                    fmv(o_fm, oc, T)[:, qt * 128:(qt + 1) * 128], psO[:], AF.Copy), se)
                pair_oev[pidx] = sv
                pidx += 1
        P.wait("tensor", se, pair_oev[pidx - 1])
        P.wait("tensor", ve, pair_ptsve[pidx - 1])
        es_at.close()
    es_grp.close()

    if dbg is not None:
        P = g.P
        if "y" in dbg:
            yo = g.out("dbg_y", [D, TH], BF16)
            P.dma("sync", yo, y_dram, g.st)
        for nm, tens in [("o", o_fm), ("qg", qg), ("kg", kg), ("vtm", vtm),
                         ("ptm0", ptm[0]), ("pts0", pts[0]),
                         ("mrow", mrow), ("lrow", lrow)]:
            if nm in dbg:
                shp = [128, tens[:].shape[1]]
                od = g.out("dbg_" + nm, shp, tens[:].dtype)
                P.wait("sync", g.se, g.se.val)
                P.wait("sync", g.ve, g.ve.val)
                P.dma("sync", od, tens[:], g.st)

    es_wo = ExitStack()
    ws = WStream(g, "wo", woT, D, D, blk=1, es=es_wo)
    xres = [g.sb(f"att_xres{i}", [128, T], F32, es_wo) for i in range(2)]
    xout = [g.sb(f"att_xout{i}", [128, T], F32, es_wo) for i in range(2)]
    ps0 = g.ps("att_ps0", [128, 512], F32, es_wo)
    ps1 = g.ps("att_ps1", [128, 512], F32, es_wo)
    ps2 = g.ps("att_ps2", [128, 512], F32, es_wo)
    ps3 = g.ps("att_ps3", [128, 512], F32, es_wo)
    te_start = te.val
    consumed = [te_start + (b + 1) * ws.blk for b in range(ws.nblk)]
    ws.issue_loads(te, consumed)
    xv = xsrc.rearrange("(c p) t -> c p t", p=128)
    dvv = xdst.rearrange("(c p) t -> c p t", p=128)
    ev_dc = {}
    st_dc = {}
    for dc in range(DC):
        pb = (ps0, ps1) if dc % 2 == 0 else (ps2, ps3)
        if dc >= 2:
            P.wait("sync", ve, ev_dc[dc - 2][0])
        P.dma("sync", xres[dc % 2][:], xv[dc, :, HALO:TH], ld)
        res_ld = ld.val
        b = dc // ws.blk
        P.wait("tensor", wld, ws.loaded[b])
        if dc >= 2:
            P.wait("tensor", ve, ev_dc[dc - 2][1])
        for k in range(DC):
            tile, lv = ws.tile(dc, k)
            for half in range(2):
                last = (k == DC - 1)
                P.op("tensor", lambda t, tile=tile, half=half, last=last, pb=pb, k=k: t.matmul(
                    pb[half][:], tile,
                    fmv(o_fm, k, T)[:, half * 512:(half + 1) * 512],
                    start=(k == 0), stop=last), te if (last and half == 1) else None)
        te_dc = te.val
        P.wait("vector", ld, res_ld)
        P.wait("vector", te, te_dc)
        if dc >= 2:
            P.wait("vector", st, st_dc[dc - 2])
        e0 = P.op("vector", lambda v, dc=dc, pb=pb: v.tensor_add(
            xout[dc % 2][:, 0:512], xres[dc % 2][:, 0:512], pb[0][:]), ve)
        e1 = P.op("vector", lambda v, dc=dc, pb=pb: v.tensor_add(
            xout[dc % 2][:, 512:1024], xres[dc % 2][:, 512:1024], pb[1][:]), ve)
        ev_dc[dc] = (e0, e1)
        P.wait("sync", ve, e1)
        P.dma("sync", dvv[dc], xout[dc % 2][:], st)
        st_dc[dc] = st.val
    es_wo.close()
    es.close()
    return st.val


def host_l0attn_weights(inp):
    return {
        "l0a_wqT": _ct(np.asarray(inp["L0_attn_wq"]).T),
        "l0a_wkT": _ct(np.asarray(inp["L0_attn_wk"]).T),
        "l0a_wvT": _ct(np.asarray(inp["L0_attn_wv"]).T),
        "l0a_woT": _ct(np.asarray(inp["L0_attn_wo"]).T),
    }


def host_l0attn_masks(core):
    b, j = divmod(core, 4)
    t0 = j * T
    m = np.zeros((NTC, 5, 128, 128), dtype=np.float32)
    for qt in range(NTC):
        gq = t0 + qt * 128 + np.arange(128)[:, None]
        for kt in range(5):
            pk = (qt + kt) * 128 + np.arange(128)[None, :]
            gk = t0 - HALO + pk
            keep = (gk <= gq) & (gq - gk < HALO) & (gk >= 0)
            m[qt, kt] = np.where(keep, 0.0, NEG)
    return {"l0a_maskS": m}


def shard_stream_halo(x):
    out = []
    for core in range(NCORES):
        b, j = divmod(core, 4)
        t0 = j * T
        sl = np.zeros((TH, D), dtype=np.float32)
        lo = max(0, t0 - HALO)
        sl[HALO - (t0 - lo):HALO] = x[b, lo:t0, :]
        sl[HALO:] = x[b, t0:t0 + T, :]
        out.append(_ct(sl.T))
    return out



INTER = 8192
CONV_DIM = 10240
PROJ = 18560
NHEAD = 128
HP = 64
NGRP = 8
NSTATE = 128
NCH = 8


def stage_l1a(g, consts, xsrc, gate_tm, hbcd_f, dt32=None):
    P = g.P
    ld, wld, st, te, ve, se = g.ld, g.wld, g.st, g.te, g.ve, g.se
    wpT = g.in_("l1a_wpT", [D, PROJ], F32)

    es = ExitStack()
    y = g.sb("m1a_y", [128, DC * T], BF16, es)
    ps_a = g.ps("m1a_psa", [1, 256], F32, es)
    ps_b = g.ps("m1a_psb", [128, 256], F32, es)
    ln_done = emit_ln_stream(g, consts, xsrc, T, y, ps_a, ps_b)
    ln_all = ln_done[-1]

    es_a = ExitStack()
    nHB = (PROJ - INTER) // 128
    ws = WStream(g, "wp", wpT[:, INTER:PROJ], D, nHB * 128, blk=1, es=es_a)
    stg = [g.sb(f"m1a_st{i}", [128, T], BF16, es_a) for i in range(2)]
    stg32 = g.sb("m1a_st32", [128, T], F32, es_a)
    ps0 = g.ps("m1a_ps0", [128, 512], F32, es_a)
    ps1 = g.ps("m1a_ps1", [128, 512], F32, es_a)
    ps2 = g.ps("m1a_ps2", [128, 512], F32, es_a)
    ps3 = g.ps("m1a_ps3", [128, 512], F32, es_a)
    P.wait("gpsimd", ve, ln_all)
    te_start = te.val
    consumed = [te_start + (b + 1) * ws.blk for b in range(ws.nblk)]
    ws.issue_loads(te, consumed)
    hv = hbcd_f.rearrange("(c p) t -> c p t", p=128)
    ev_dc = {}
    st_dc = {}
    P.wait("tensor", ve, ln_all)
    for dc in range(nHB):
        pb = (ps0, ps1) if dc % 2 == 0 else (ps2, ps3)
        P.wait("tensor", wld, ws.loaded[dc])
        if dc >= 2:
            P.wait("tensor", ve, ev_dc[dc - 2])
        for k in range(DC):
            tile, lv = ws.tile(dc, k)
            for half in range(2):
                last = (k == DC - 1)
                P.op("tensor", lambda t, tile=tile, half=half, last=last, pb=pb, k=k: t.matmul(
                    pb[half][:], tile,
                    fmv(y, k, T)[:, half * 512:(half + 1) * 512],
                    start=(k == 0), stop=last), te if (last and half == 1) else None)
        te_dc = te.val
        P.wait("vector", te, te_dc)
        if dc >= 2:
            P.wait("vector", st, st_dc[dc - 2])
        P.op("vector", lambda v, dc=dc, pb=pb: v.tensor_copy(stg[dc % 2][:, 0:512], pb[0][:]))
        ev = P.op("vector", lambda v, dc=dc, pb=pb: v.tensor_copy(stg[dc % 2][:, 512:1024], pb[1][:]), ve)
        if dc == nHB - 1 and dt32 is not None:
            P.op("vector", lambda v, pb=pb: v.tensor_copy(stg32[:, 0:512], pb[0][:]))
            ev = P.op("vector", lambda v, pb=pb: v.tensor_copy(stg32[:, 512:1024], pb[1][:]), ve)
        ev_dc[dc] = ev
        P.wait("sync", ve, ev)
        P.dma("sync", hv[dc], stg[dc % 2][:], st)
        if dc == nHB - 1 and dt32 is not None:
            P.dma("sync", dt32, stg32[:], st)
        st_dc[dc] = st.val
    a_done_te = te.val
    a_done_st = st.val
    es_a.close()

    es_b = ExitStack()
    NCG = 16
    wg = [g.sb(f"m1a_wg{i}", [128, DC * 512], BF16, es_b) for i in range(2)]
    gstg = [g.sb(f"m1a_gs{i}", [128, 512], BF16, es_b) for i in range(2)]
    psg = [g.ps(f"m1a_pg{i}", [128, 512], F32, es_b) for i in range(2)]
    P.wait("gpsimd", te, a_done_te)
    P.wait("gpsimd", st, a_done_st)
    gate_te0 = te.val
    for cg in range(NCG):
        if cg >= 2:
            P.wait("gpsimd", te, gate_te0 + (cg - 1) * NTC, future=True)
        P.dma("gpsimd", wg[cg % 2][:].rearrange("p (k c) -> p k c", c=512),
              wpT[:, cg * 512:(cg + 1) * 512].rearrange("(k p) c -> p k c", p=128), wld)
        cg_ld = wld.val
        P.wait("tensor", wld, cg_ld)
        for tt in range(NTC):
            if cg > 0 or tt >= 2:
                idx = cg * NTC + tt
                P.wait("tensor", ve, ev_tt[idx - 2] if idx >= 2 else 0)
            for k in range(DC):
                tev = P.op("tensor", lambda t, cg=cg, tt=tt, k=k: t.matmul(
                    psg[tt % 2][:],
                    fmv(y, k, T)[:, tt * 128:(tt + 1) * 128],
                    wg[cg % 2][:].rearrange("p (k c) -> p k c", c=512)[:, k, :],
                    start=(k == 0), stop=(k == DC - 1)), te if k == DC - 1 else None)
            idx = cg * NTC + tt
            P.wait("vector", te, te.val)
            if idx >= 2:
                P.wait("vector", st, st_tt[idx - 2])
            if cg == 0 and tt == 0:
                ev_tt = {}
                st_tt = {}
            ev = P.op("vector", lambda v, tt=tt: v.tensor_copy(gstg[tt % 2][:], psg[tt % 2][:]), ve)
            ev_tt[idx] = ev
            P.wait("sync", ve, ev)
            gv = gate_tm.rearrange("(tt p) c -> tt p c", p=128)
            P.dma("sync", gv[tt, :, cg * 512:(cg + 1) * 512], gstg[tt % 2][:], st)
            st_tt[idx] = st.val
    es_b.close()
    es.close()
    return st.val


def host_l1a_weights(inp):
    return {"l1a_wpT": _ct(np.asarray(inp["L1_attn_in_proj_w"]).T)}



def stage_l1b(g, consts, hbcd_f, halo3, dt32, b_dram, c_dram, y_tm, s_out, a_cs_out, asum_out):
    P = g.P
    ld, wld, st, te, ve, se = g.ld, g.wld, g.st, g.te, g.ve, g.se
    dt_bias = g.in_("l1b_dtb", [128, 1], F32)
    A_log = g.in_("l1b_alog", [128, 1], F32)
    d_exp_in = g.in_("l1b_dexp", [128, INTER], F32)
    conv_w = g.in_("l1b_cw", [128, 80 * 4], F32)
    conv_b = g.in_("l1b_cb", [128, 80], F32)
    masktri = g.in_("l1b_mtri", [128, 512], F32)

    hs_tm = g.dram("l1b_hstm", [T, INTER], BF16)

    es = ExitStack()
    b_tm = g.sb("m1b_btm", [128, NCH * NGRP * 128], BF16, es)
    s_prev = g.sb("m1b_sprev", [128, INTER], F32, es)
    sprev_b = g.sb("m1b_sprevb", [128, INTER], BF16, es)
    d_exp = g.sb("m1b_dexp", [128, INTER], BF16, es)
    hsd = g.sb("m1b_hsd", [128, INTER], BF16, es)
    ystg = g.sb("m1b_ystg", [128, INTER], BF16, es)
    hsbuf = g.sb("m1b_hsbuf", [128, INTER], BF16, es)
    a_cs = g.sb("m1b_acs", [128, T], F32, es)
    natm = g.sb("m1b_natm", [128, NCH * 128], F32, es)
    e_tm = g.sb("m1b_etm", [128, NCH * 128], F32, es)
    dt_tm = g.sb("m1b_dttm", [128, NCH * 128], F32, es)
    w_tm = g.sb("m1b_wtm", [128, NCH * 128], F32, es)
    cw_s = g.sb("m1b_cw", [128, 320], F32, es)
    cb_s = g.sb("m1b_cb", [128, 80], F32, es)
    dtb_s = g.sb("m1b_dtb", [128, 2], F32, es)
    nega = g.sb("m1b_nega", [128, 2], F32, es)
    mtri_s = g.sb("m1b_mtri", [128, 512], F32, es)
    dec_sb = g.sb("m1b_dec", [128, 128], F32, es)
    gmt_sb = g.sb("m1b_gmt", [128, NGRP * 128], BF16, es)
    bslab = g.sb("m1b_bsl", [128, NGRP * 128], BF16, es)
    cslab = g.sb("m1b_csl", [128, NGRP * 128], BF16, es)
    e_sb = [g.sb(f"m1b_esb{i}", [128, 8 * 128], BF16, es) for i in range(2)]
    m_sb = [g.sb(f"m1b_msb{i}", [128, 8 * 128], BF16, es) for i in range(2)]
    bp_sb = [g.sb(f"m1b_bpsb{i}", [128, 8 * 128], BF16, es) for i in range(2)]
    yd_sb = [g.sb(f"m1b_ydsb{i}", [128, 512], F32, es) for i in range(2)]
    yo_sb = [g.sb(f"m1b_yosb{i}", [128, 512], F32, es) for i in range(2)]
    asum_s = g.sb("m1b_asum", [128, 1], F32, es)

    P.dma("sync", dtb_s[:, 0:1], dt_bias, ld)
    P.dma("sync", nega[:, 0:1], A_log, ld)
    P.dma("sync", cw_s[:], conv_w, ld)
    P.dma("sync", cb_s[:], conv_b, ld)
    P.dma("sync", mtri_s[:], masktri, ld)
    v_ld = ld.val
    P.dma("gpsimd", d_exp[:], d_exp_in, wld)
    dexp_wld = wld.val

    hvv = hbcd_f.rearrange("(c p) t -> c p t", p=128)

    es_dt = ExitStack()
    dt_f = g.sb("m1b_dt", [128, T], F32, es_dt)
    scr = g.sb("m1b_scr", [128, T], F32, es_dt)
    scr2 = g.sb("m1b_scr2", [128, T], F32, es_dt)
    ps_t1 = g.ps("m1b_pst1", [128, 128], F32, es_dt)
    ps_t2 = g.ps("m1b_pst2", [128, 128], F32, es_dt)
    P.dma("sync", scr[:], dt32, ld)
    dt_ld = ld.val
    P.wait("vector", ld, dt_ld)
    P.op("vector", lambda v: v.tensor_scalar(
        scr[:], scr[:], dtb_s[:, 0:1], 1.0, ALU.add, ALU.mult))
    vv = P.op("vector", lambda v: v.engine_nop(), ve)
    P.wait("scalar", ve, vv)
    sv = P.op("scalar", lambda s: s.activation(scr2[:], scr[:], AF.Abs), se)
    sv = P.op("scalar", lambda s: s.activation(scr2[:], scr2[:], AF.Exp, scale=-1.0), se)
    sv = P.op("scalar", lambda s: s.activation(scr2[:], scr2[:], AF.Ln, bias=1.0), se)
    P.wait("vector", se, sv)
    P.op("vector", lambda v: v.tensor_scalar_max(dt_f[:], scr[:], 0.0))
    vv = P.op("vector", lambda v: v.tensor_add(dt_f[:], dt_f[:], scr2[:]), ve)
    P.wait("vector", ve, vv)
    P.wait("scalar", ve, vv)
    sv = P.op("scalar", lambda s: s.activation(nega[:, 0:1], nega[:, 0:1], AF.Exp), se)
    P.wait("vector", se, sv)
    vv = P.op("vector", lambda v: v.tensor_scalar_mul(nega[:, 0:1], nega[:, 0:1], -1.0), ve)
    P.wait("vector", ve, vv)
    vv = P.op("vector", lambda v: v.tensor_scalar_mul(scr[:], dt_f[:], nega[:, 0:1]), ve)
    P.wait("vector", ve, vv)
    for c in range(NCH):
        sl = slice(c * 128, (c + 1) * 128)
        vv = P.op("vector", lambda v, sl=sl: v.tensor_tensor_scan(
            a_cs[:, sl], consts["ones_t"][:], scr[:, sl], 0.0, ALU.mult, ALU.add), ve)
    P.wait("vector", ve, vv)
    acs_ve = vv
    P.wait("scalar", ve, acs_ve)
    for c in range(NCH):
        sl = slice(c * 128, (c + 1) * 128)
        sv = P.op("scalar", lambda s, sl=sl, c=c: s.activation(
            scr2[:, sl], a_cs[:, sl], AF.Exp, scale=-1.0,
            bias=a_cs[:, c * 128 + 127:c * 128 + 128]), se)
    P.wait("vector", se, sv)
    vv = P.op("vector", lambda v: v.tensor_mul(scr2[:], scr2[:], dt_f[:]), ve)
    vv2 = P.op("vector", lambda v: v.tensor_scalar_mul(scr[:], a_cs[:], -1.0), ve)
    P.wait("vector", ve, vv2)
    avw = a_cs[:].rearrange("p (c t) -> p c t", t=128)[:, :, 127:128].squeeze(2)
    vv3 = P.op("vector", lambda v: v.reduce_sum(asum_s[:], avw, AX.X), ve)
    P.wait("tensor", ve, vv2)
    tr_jobs = [(scr, natm, "v"), (a_cs, e_tm, "exp"), (dt_f, dt_tm, "v"), (scr2, w_tm, "v")]
    pst = [ps_t1, ps_t2]
    tidx = 0
    ev_prev = {}
    for (srcT, dstT, mode) in tr_jobs:
        for c in range(NCH):
            pb = pst[tidx % 2]
            if tidx >= 2:
                eng, v_ = ev_prev[tidx - 2]
                P.wait("tensor", g.ve if eng == "v" else g.se, v_)
            tev = P.op("tensor", lambda t, srcT=srcT, c=c, pb=pb: t.transpose(
                pb[:], srcT[:, c * 128:(c + 1) * 128], consts["ident_f"][:]), te)
            dsl = dstT[:, c * 128:(c + 1) * 128]
            if mode == "exp":
                P.wait("scalar", te, tev)
                sv = P.op("scalar", lambda s, dsl=dsl, pb=pb: s.activation(
                    dsl, pb[:], AF.Exp), se)
                ev_prev[tidx] = ("s", sv)
            else:
                P.wait("vector", te, tev)
                vv = P.op("vector", lambda v, dsl=dsl, pb=pb: v.tensor_copy(dsl, pb[:]), ve)
                ev_prev[tidx] = ("v", vv)
            tidx += 1
    tm_done_v = ve.val
    tm_done_s = se.val
    es_dt.close()

    es_cv = ExitStack()
    hcin = [g.sb(f"m1b_hc{i}", [128, 1027], BF16, es_cv) for i in range(2)]
    acc = [g.sb(f"m1b_acc{i}", [128, T], F32, es_cv) for i in range(2)]
    hstg = [g.sb(f"m1b_hst{i}", [128, T], BF16, es_cv) for i in range(2)]
    tstg = [g.sb(f"m1b_tst{i}", [128, T], BF16, es_cv) for i in range(2)]
    ps_c = [g.ps(f"m1b_psc{i}", [128, 128], BF16, es_cv) for i in range(2)]
    cc_ld = {}
    cc_done = {}
    hv3 = halo3.rearrange("(c p) k -> c p k", p=128)
    P.wait("sync", ve, tm_done_v)
    P.wait("sync", se, tm_done_s)
    for cc in range(80):
        if cc >= 2:
            eng_sem, v_ = cc_done[cc - 2]
            P.wait("sync", eng_sem, v_, future=True)
        P.dma("sync", hcin[cc % 2][:, 0:3], hv3[cc], ld)
        P.dma("sync", hcin[cc % 2][:, 3:1027], hvv[cc], ld)
        ldv = ld.val
        P.wait("scalar", ld, ldv)
        if cc == 0:
            P.wait("scalar", ld, v_ld)
            P.wait("vector", ld, v_ld)
            P.wait("gpsimd", ld, v_ld)
        if cc >= 2:
            if cc - 2 < 72:
                P.wait("scalar", te, tr_te.get(cc - 2, 0), future=True)
            if 64 <= cc - 2 < 80:
                P.wait("scalar", st, bc_st.get(cc - 2, 0), future=True)
        sv = P.op("scalar", lambda s, cc=cc: s.activation(
            acc[cc % 2][:], hcin[cc % 2][:, 3:1027], AF.Copy,
            scale=cw_s[:, cc * 4 + 3:cc * 4 + 4]), se)
        eng = "vector"
        esem = ve
        P.wait(eng, se, sv)
        for j in (2, 1, 0):
            vv = P.op(eng, lambda e, cc=cc, j=j: e.scalar_tensor_tensor(
                acc[cc % 2][:], hcin[cc % 2][:, j:j + 1024],
                cw_s[:, cc * 4 + j:cc * 4 + j + 1], acc[cc % 2][:],
                ALU.mult, ALU.add), esem)
            P.wait(eng, esem, vv)
        tap_v = vv
        P.wait("scalar", esem, tap_v)
        if cc == 0:
            cc_sil = {}
        dst = hstg[cc % 2][:]
        sv = P.op("scalar", lambda s, cc=cc, dst=dst: s.activation(
            dst, acc[cc % 2][:], AF.Silu, bias=cb_s[:, cc:cc + 1]), se)
        cc_sil[cc] = sv
        cc_done[cc] = (esem, tap_v)
        if cc == 0:
            tr_te = {}
            bc_st = {}
            hs_st = {}
        if 64 <= cc < 80:
            P.wait("sync", se, sv)
            tgt = b_dram if cc < 72 else c_dram
            row = (cc - 64) if cc < 72 else (cc - 72)
            tv = tgt.rearrange("(c p) t -> c p t", p=128)
            P.dma("sync", tv[row], hstg[cc % 2][:], st)
            bc_st[cc] = st.val
        if cc < 72:
            P.wait("tensor", se, sv)
            if cc >= 2 and cc - 2 < 64:
                P.wait("vector", st, hs_st.get(cc - 2, 0), future=True)
            for tc in range(NCH):
                if not (cc == 0 and tc < 2):
                    pe, pv = tr_ev[(cc * NCH + tc) - 2] if (cc * NCH + tc) >= 2 else (None, None)
                    if pe is not None:
                        P.wait("tensor", pe, pv)
                tev = P.op("tensor", lambda t, cc=cc, tc=tc: t.transpose(
                    ps_c[tc % 2][:], hstg[cc % 2][:, tc * 128:(tc + 1) * 128],
                    consts["ident_b"][:]), te)
                if cc == 0 and tc == 0:
                    tr_ev = {}
                P.wait("vector", te, tev)
                if cc < 64:
                    dsl = tstg[cc % 2][:, tc * 128:(tc + 1) * 128]
                else:
                    dsl = b_tm[:, (tc * NGRP + (cc - 64)) * 128:(tc * NGRP + (cc - 64) + 1) * 128]
                vv = P.op("vector", lambda v, dsl=dsl, tc=tc: v.tensor_copy(
                    dsl, ps_c[tc % 2][:]), ve)
                tr_ev[cc * NCH + tc] = (ve, vv)
                if tc == NCH - 1:
                    tr_te[cc] = te.val
            if cc < 64:
                P.wait("sync", ve, vv)
                hsv = hs_tm.rearrange("(tc p) ch -> tc p ch", p=128)
                P.dma("sync",
                      hsv[:, :, cc * 128:(cc + 1) * 128].transpose([1, 0, 2]),
                      tstg[cc % 2][:].rearrange("p (tc ch) -> p tc ch", ch=128),
                      st)
                hs_st[cc] = st.val
    conv_st = st.val
    conv_btm_ve = ve.val
    es_cv.close()

    es_sd = ExitStack()
    psTT = [g.ps(f"m1b_psTT{i}", [128, 512], F32, es_sd) for i in range(2)]
    psY = [g.ps(f"m1b_psY{i}", [128, 512], F32, es_sd) for i in range(2)]
    psYo = [g.ps(f"m1b_psYo{i}", [128, 512], F32, es_sd) for i in range(2)]
    psSt = [g.ps(f"m1b_psSt{i}", [128, 512], F32, es_sd) for i in range(2)]
    P.op("vector", lambda v: v.memset(s_prev[:], 0.0), ve)
    sprev_init = ve.val
    P.wait("sync", st, conv_st)
    bvv = b_dram.rearrange("(gc p) t -> gc p t", p=128)
    cvv = c_dram.rearrange("(gc p) t -> gc p t", p=128)
    wave_oev = {}
    wave_mpv = {}
    wave_ste = {}
    ychunk_st = {}
    for c in range(NCH):
        if c >= 1:
            P.wait("sync", te, hs_cons[c - 1], future=True)
        else:
            hs_cons = {}
        P.dma("sync", hsbuf[:], hs_tm[c * 128:(c + 1) * 128, :], ld)
        P.dma("sync", bslab[:].rearrange("p (g t) -> p g t", t=128),
              bvv[:, :, c * 128:(c + 1) * 128].transpose([1, 0, 2]), ld)
        P.dma("sync", cslab[:].rearrange("p (g t) -> p g t", t=128),
              cvv[:, :, c * 128:(c + 1) * 128].transpose([1, 0, 2]), ld)
        c_ld = ld.val
        if c >= 1:
            P.wait("gpsimd", st, ychunk_st[c - 1])
        P.wait("tensor", ld, c_ld)
        if c == 0:
            P.wait("tensor", ve, sprev_init)
            P.wait("tensor", ve, tm_done_v)
            P.wait("tensor", se, tm_done_s)
            P.wait("tensor", wld, dexp_wld)
            P.wait("tensor", ve, conv_btm_ve)
        else:
            P.wait("tensor", ve, wave_oev[(c - 1, 15)][0])
            P.wait("tensor", se, wave_oev[(c - 1, 15)][1])
            P.wait("tensor", ve, wave_ste[(c - 1, 15)])
        for gq in range(NGRP):
            tev = P.op("tensor", lambda t, gq=gq: t.matmul(
                psTT[gq // 4][:, (gq % 4) * 128:(gq % 4 + 1) * 128],
                bslab[:].rearrange("p (g t) -> p g t", t=128)[:, gq, :],
                cslab[:].rearrange("p (g t) -> p g t", t=128)[:, gq, :],
                start=True, stop=True), te if gq == NGRP - 1 else None)
        P.wait("vector", te, tev)
        P.op("vector", lambda v: v.tensor_copy(gmt_sb[:, 0:512], psTT[0][:]))
        gmt_ve = P.op("vector", lambda v: v.tensor_copy(gmt_sb[:, 512:1024], psTT[1][:]), ve)
        P.wait("tensor", ve, gmt_ve)
        lhs_dec = consts["ident_f"][:, 127:128].broadcast_to([128, 128])
        tev = P.op("tensor", lambda t, c=c, lhs_dec=lhs_dec: t.matmul(
            psTT[0][:, 0:128], lhs_dec, e_tm[:, c * 128:(c + 1) * 128],
            start=True, stop=True), te)
        P.wait("vector", te, tev)
        dec_ve = P.op("vector", lambda v: v.tensor_copy(dec_sb[:], psTT[0][:, 0:128]), ve)
        sprevb_ve = P.op("vector", lambda v: v.tensor_copy(sprev_b[:], s_prev[:]), ve)
        P.wait("gpsimd", ld, c_ld)
        if c == 0:
            P.wait("gpsimd", wld, dexp_wld)
        if c >= 1:
            P.wait("gpsimd", te, hsd_cons, future=True)
        hsd_ge = P.op("gpsimd", lambda e: e.tensor_tensor(
            hsd[:], hsbuf[:], d_exp[:], ALU.mult), g.ge)
        P.wait("vector", ve, dec_ve)
        decv = dec_sb[:].unsqueeze(2).broadcast_to([128, 128, HP])
        spv = s_prev[:].rearrange("p (h q) -> p h q", q=HP)
        sdec_ve = P.op("vector", lambda v, decv=decv, spv=spv: v.tensor_tensor(
            spv, spv, decv, ALU.mult), ve)
        for w in range(16):
            wp = w % 2
            h0 = w * 8
            if w >= 1:
                P.wait("tensor", se, wave_expse[(c, w - 1)], future=True)
            elif c == 0 and w == 0:
                wave_expse = {}
                wave_te = {}
            if w == 0:
                P.wait("tensor", ve, dec_ve)
            for bank in range(2):
                P.op("tensor", lambda t, bank=bank: t.matmul(
                    psTT[bank][:], consts["ident_f"][:], mtri_s[:],
                    start=True, stop=False))
                for hh in range(4):
                    h = h0 + bank * 4 + hh
                    lhsb = consts["ident_f"][:, h:h + 1].broadcast_to([128, 128])
                    tev = P.op("tensor", lambda t, bank=bank, hh=hh, lhsb=lhsb, c=c: t.matmul(
                        psTT[bank][:, hh * 128:(hh + 1) * 128], lhsb,
                        a_cs[:, c * 128:(c + 1) * 128],
                        start=False, stop=(hh == 3)), te if (bank == 1 and hh == 3) else None)
            tt_te = tev
            P.wait("scalar", te, tt_te)
            if w >= 2:
                P.wait("scalar", ve, wave_mpv[(c, w - 2)][0], future=True)
            for hh in range(8):
                h = h0 + hh
                sv = P.op("scalar", lambda s, wp=wp, hh=hh, h=h, c=c: s.activation(
                    e_sb[wp][:, hh * 128:(hh + 1) * 128],
                    psTT[hh // 4][:, (hh % 4) * 128:(hh % 4 + 1) * 128],
                    AF.Exp, bias=natm[:, c * 128 + h:c * 128 + h + 1]), se)
            wave_expse[(c, w)] = sv
            P.wait("vector", se, sv)
            if w == 0:
                P.wait("vector", ve, sprevb_ve)
            if not (c == 0 and w < 2):
                pc, pw = (c, w - 2) if w >= 2 else (c - 1, 14 + w)
                P.wait("vector", te, wave_te[(pc, pw)], future=True)
            last_v = last_g = None
            for hh in range(8):
                h = h0 + hh
                gq = h % NGRP
                eng = "vector"
                sm = ve
                mdst = m_sb[wp][:, hh * 128:(hh + 1) * 128]
                esrc = e_sb[wp][:, hh * 128:(hh + 1) * 128]
                vv = P.op(eng, lambda e, mdst=mdst, esrc=esrc, c=c, h=h, gq=gq: e.scalar_tensor_tensor(
                    mdst, esrc, dt_tm[:, c * 128 + h:c * 128 + h + 1],
                    gmt_sb[:, gq * 128:(gq + 1) * 128], ALU.mult, ALU.mult), sm)
                bdst = bp_sb[wp][:, hh * 128:(hh + 1) * 128]
                btile = b_tm[:, (c * NGRP + gq) * 128:(c * NGRP + gq + 1) * 128]
                vv = P.op(eng, lambda e, bdst=bdst, btile=btile, c=c, h=h: e.tensor_scalar_mul(
                    bdst, btile, w_tm[:, c * 128 + h:c * 128 + h + 1]), sm)
                if eng == "vector":
                    last_v = vv
                else:
                    last_g = vv
            wave_mpv[(c, w)] = (last_v, last_g)
            P.wait("tensor", ve, last_v)
            if w == 0:
                P.wait("tensor", g.ge, hsd_ge)
            if not (c == 0 and w < 2):
                pc, pw = (c, w - 2) if w >= 2 else (c - 1, 14 + w)
                P.wait("tensor", ve, wave_oev[(pc, pw)][0], future=True)
                P.wait("tensor", se, wave_oev[(pc, pw)][1], future=True)
                P.wait("tensor", ve, wave_ste[(pc, pw)], future=True)
            for hh in range(8):
                h = h0 + hh
                gq = h % NGRP
                sl = slice(hh * 64, (hh + 1) * 64)
                P.op("tensor", lambda t, wp=wp, hh=hh, h=h, sl=sl: t.matmul(
                    psY[wp][:, sl], m_sb[wp][:, hh * 128:(hh + 1) * 128],
                    hsbuf[:, h * 64:(h + 1) * 64], start=(hh == 0), stop=False))
                P.op("tensor", lambda t, wp=wp, hh=hh, h=h, gq=gq, sl=sl: t.matmul(
                    psYo[wp][:, sl],
                    cslab[:].rearrange("p (g t) -> p g t", t=128)[:, gq, :],
                    sprev_b[:, h * 64:(h + 1) * 64], start=(hh == 0), stop=(hh == 7)))
                tev = P.op("tensor", lambda t, wp=wp, hh=hh, h=h, sl=sl: t.matmul(
                    psSt[wp][:, sl], bp_sb[wp][:, hh * 128:(hh + 1) * 128],
                    hsbuf[:, h * 64:(h + 1) * 64], start=(hh == 0), stop=(hh == 7)),
                    te if hh == 7 else None)
            tev = P.op("tensor", lambda t, wp=wp, h0=h0: t.matmul(
                psY[wp][:], consts["ident_b"][:], hsd[:, h0 * 64:(h0 + 8) * 64],
                start=False, stop=True), te)
            wave_te[(c, w)] = tev
            if w == 15:
                hs_cons[c] = tev
                hsd_cons = tev
            P.wait("scalar", te, wave_te[(c, w)])
            if not (c == 0 and w < 2):
                pc, pw = (c, w - 2) if w >= 2 else (c - 1, 14 + w)
                P.wait("scalar", g.ge, wave_oev[(pc, pw)][2], future=True)
            sv = P.op("scalar", lambda s, wp=wp: s.activation(
                yd_sb[wp][:], psY[wp][:], AF.Copy), se)
            ydiag_se = sv
            P.wait("vector", te, wave_te[(c, w)])
            if not (c == 0 and w < 2):
                P.wait("vector", g.ge, wave_oev[(pc, pw)][2], future=True)
            ev = e_tm[:, c * 128 + h0:c * 128 + h0 + 8].unsqueeze(2).broadcast_to([128, 8, 64])
            vv = P.op("vector", lambda v, wp=wp, ev=ev: v.tensor_tensor(
                yo_sb[wp][:].rearrange("p (h q) -> p h q", q=64),
                psYo[wp][:].rearrange("p (h q) -> p h q", q=64), ev, ALU.mult), ve)
            P.wait("gpsimd", ve, vv)
            P.wait("gpsimd", se, ydiag_se)
            gv = P.op("gpsimd", lambda e, wp=wp, h0=h0: e.tensor_add(
                ystg[:, h0 * 64:(h0 + 8) * 64], yd_sb[wp][:], yo_sb[wp][:]), g.ge)
            wave_oev[(c, w)] = (vv, ydiag_se, gv)
            if w == 0:
                P.wait("vector", ve, sdec_ve)
            vv = P.op("vector", lambda v, wp=wp, h0=h0: v.tensor_add(
                s_prev[:, h0 * 64:(h0 + 8) * 64], s_prev[:, h0 * 64:(h0 + 8) * 64],
                psSt[wp][:]), ve)
            wave_ste[(c, w)] = vv
        lastg = wave_oev[(c, 15)][2]
        P.wait("sync", g.ge, lastg)
        if c >= 2:
            P.wait("sync", st, ychunk_st[c - 2], future=True)
        P.dma("sync", y_tm[c * 128:(c + 1) * 128, :], ystg[:], st)
        ychunk_st[c] = st.val
    if getattr(g, "dbg2b", False):
        P.wait("sync", se, se.val)
        P.wait("sync", ve, ve.val)
        P.wait("sync", g.ge, g.ge.val)
        for nm, tens in [("natm", natm), ("etm", e_tm), ("dttm", dt_tm), ("wtm", w_tm),
                         ("btm", b_tm), ("hsd", hsd), ("sprevb", sprev_b),
                         ("gmt", gmt_sb), ("esb0", e_sb[0]), ("esb1", e_sb[1]),
                         ("msb0", m_sb[0]), ("msb1", m_sb[1]),
                         ("bpsb1", bp_sb[1]), ("ydsb1", yd_sb[1]), ("yosb1", yo_sb[1]),
                         ("ystg", ystg), ("dec", dec_sb), ("hsbuf", hsbuf)]:
            od = g.out("dbg_" + nm, [128, tens[:].shape[1]], tens[:].dtype)
            P.dma("sync", od, tens[:], g.st)
    P.wait("sync", ve, wave_ste[(NCH - 1, 15)])
    P.dma("sync", s_out, s_prev[:], st)
    P.wait("sync", ve, max( acs_ve, vv3 ))
    P.dma("sync", a_cs_out, a_cs[:], st)
    P.dma("sync", asum_out, asum_s[:], st)
    es_sd.close()
    es.close()
    return st.val


def host_l1b_weights(inp):
    cw = np.asarray(inp["L1_attn_conv_w"]).reshape(10240, 4)
    cb = np.asarray(inp["L1_attn_conv_b"]).reshape(80, 128)
    dexp = np.repeat(np.asarray(inp["L1_attn_D"]).astype(np.float32), HP)
    mtri = np.where(np.triu(np.ones((128, 128), bool)), 0.0, -1.0e30).astype(np.float32)
    return {
        "l1b_dtb": _ct(np.asarray(inp["L1_attn_dt_bias"]).reshape(128, 1)),
        "l1b_alog": _ct(np.asarray(inp["L1_attn_A_log"]).reshape(128, 1)),
        "l1b_dexp": _ct(np.broadcast_to(dexp[None, :], (128, INTER)).copy()),
        "l1b_cw": _ct(cw.reshape(80, 128, 4).transpose(1, 0, 2).reshape(128, 320)),
        "l1b_cb": _ct(cb.T),
        "l1b_mtri": _ct(np.tile(mtri, (1, 4))),
    }



def stage_l1c(g, consts, xsrc, y_tm, c_dram, a_cs_in, gate_tm, xdst, limit=None):
    P = g.P
    ld, wld, st, te, ve, se = g.ld, g.wld, g.st, g.te, g.ve, g.se
    woT = g.in_("l1c_woT", [INTER, D], F32)
    nw_in = g.in_("l1c_nw", [128, INTER], F32)
    pS = [g.in_(f"l1c_p{i}S", [128, INTER], F32) for i in range(3)]
    pA = [g.in_(f"l1c_p{i}A", [128, 1], F32) for i in range(3)]
    psel = g.in_("l1c_sel", [128, 3], F32)

    es = ExitStack()
    nw = g.sb("m1c_nw", [128, INTER], BF16, es)
    s_acc = g.sb("m1c_sacc", [128, INTER], F32, es)
    scorr = g.sb("m1c_scorr", [128, INTER], BF16, es)
    acs = g.sb("m1c_acs", [128, T], F32, es)
    sel_s = g.sb("m1c_sel", [128, 3], F32, es)
    pa_s = g.sb("m1c_pa", [128, 4], F32, es)
    pref = g.sb("m1c_pref", [128, NCH], F32, es)
    colrow = g.sb("m1c_crow", [1, 128], F32, es)
    fb_sb = g.sb("m1c_fb", [128, 128], F32, es)
    e_ch = g.sb("m1c_ech", [128, 128], F32, es)
    nch_s = g.sb("m1c_nachs", [128, 128], F32, es)
    es_s = ExitStack()
    stmp = g.sb("m1c_stmp", [128, INTER], F32, es_s)

    ps_c = [g.ps(f"m1c_psc{i}", [128, 512], F32, es) for i in range(2)]
    ps_tr = [g.ps(f"m1c_pstr{i}", [128, 128], BF16, es) for i in range(2)]
    ps_m = g.ps("m1c_psm", [128, 128], F32, es)
    ps_r = g.ps("m1c_psr", [1, 128], F32, es)

    P.dma("gpsimd", nw[:], nw_in, wld)
    nw_wld = wld.val
    P.dma("sync", acs[:], a_cs_in, ld)
    P.dma("sync", sel_s[:], psel, ld)
    for i in range(3):
        P.dma("sync", pa_s[:, i:i + 1], pA[i], ld)
    sm_ld = ld.val

    P.dma("sync", s_acc[:], pS[0], ld)
    s0_ld = ld.val
    P.wait("vector", ld, s0_ld)
    vv = P.op("vector", lambda v: v.tensor_scalar_mul(s_acc[:], s_acc[:], sel_s[:, 0:1]), ve)
    sacc_ve = vv
    P.op("vector", lambda v: v.tensor_copy(pa_s[:, 3:4], pa_s[:, 0:1]))
    for i in (1, 2):
        P.wait("sync", ve, sacc_ve if i == 1 else sacc_ve2, future=(i == 2))
        P.dma("sync", stmp[:], pS[i], ld)
        si_ld = ld.val
        vv = P.op("vector", lambda v, i=i: v.tensor_copy(fb_sb[:, 0:1], pa_s[:, 3:4]), ve)
        P.wait("scalar", ve, vv)
        sv = P.op("scalar", lambda s: s.activation(fb_sb[:, 0:1], fb_sb[:, 0:1], AF.Exp), se)
        P.wait("vector", se, sv)
        vv = P.op("vector", lambda v, i=i: v.tensor_scalar_mul(
            fb_sb[:, 0:1], fb_sb[:, 0:1], sel_s[:, i:i + 1]), ve)
        P.wait("vector", ve, vv)
        P.wait("tensor", ve, vv)
        tev = P.op("tensor", lambda t: t.transpose(
            ps_r[:], fb_sb[:, 0:1], consts["ident_f"][:]), te)
        P.wait("vector", te, tev)
        vv = P.op("vector", lambda v: v.tensor_copy(colrow[:], ps_r[:]), ve)
        P.wait("tensor", ve, vv)
        tev = P.op("tensor", lambda t: t.matmul(
            ps_m[:], consts["ones_row_f"][0:1, 0:128], colrow[:],
            start=True, stop=True), te)
        P.wait("vector", te, tev)
        vv = P.op("vector", lambda v: v.tensor_copy(fb_sb[:], ps_m[:]), ve)
        P.wait("vector", ve, vv)
        P.wait("vector", ld, si_ld)
        fbv = fb_sb[:].unsqueeze(2).broadcast_to([128, 128, HP])
        stv = stmp[:].rearrange("p (h q) -> p h q", q=HP)
        P.op("vector", lambda v, fbv=fbv, stv=stv: v.tensor_tensor(stv, stv, fbv, ALU.mult))
        vv = P.op("vector", lambda v: v.tensor_add(s_acc[:], s_acc[:], stmp[:]), ve)
        sacc_ve2 = vv
        if i == 1:
            vv = P.op("vector", lambda v: v.tensor_add(pa_s[:, 3:4], pa_s[:, 3:4], pa_s[:, 1:2]), ve)
            P.wait("vector", ve, vv)
    sin_ve = ve.val

    if limit == "sin":
        od = g.out("dbg_sacc", [128, INTER], F32)
        P.wait("sync", ve, sin_ve)
        P.dma("sync", od, s_acc[:], g.st)
        es_s.close()
        es.close()
        return g.st.val
    es_s.close()
    normed_fm = g.sb("m1c_nfm", [128, 32 * T], BF16, es)
    P.wait("sync", ve, sin_ve)

    P.op("vector", lambda v: v.memset(pref[:, 0:1], 0.0))
    P.wait("vector", ld, sm_ld)
    for c in range(1, NCH):
        vv = P.op("vector", lambda v, c=c: v.tensor_add(
            pref[:, c:c + 1], pref[:, c - 1:c], acs[:, c * 128 - 1:c * 128]), ve)
        P.wait("vector", ve, vv)
    pref_ve = ve.val

    wov = woT.rearrange("(hh k p) c -> hh k p c", hh=2, p=128)
    cvv = c_dram.rearrange("(gc p) t -> gc p t", p=128)
    dvv = xdst.rearrange("(c p) t -> c p t", p=128)
    xsv = xsrc.rearrange("(c p) t -> c p t", p=128)
    for half in range(2):
        hoff = half * 64
        es_t = ExitStack()
        ych = g.sb("m1c_ych", [128, D], BF16, es_t)
        gch = g.sb("m1c_gch", [128, D], BF16, es_t)
        sgch = g.sb("m1c_sgch", [128, D], BF16, es_t)
        y2 = g.sb("m1c_y2", [128, D], BF16, es_t)
        yo_w = g.sb("m1c_yow", [128, 512], F32, es_t)
        ntm = g.sb("m1c_ntm", [128, D], BF16, es_t)
        cslab = g.sb("m1c_csl", [128, NGRP * 128], BF16, es_t)
        ms = g.sb("m1c_ms", [128, 4], F32, es_t)
        if half == 1:
            P.wait("sync", te, h0_mm_te)
            P.wait("sync", st, h0_last_st)
        for c in range(NCH):
            if (half, c) != (0, 0):
                P.wait("sync", te, ld_gate_te, future=True)
                P.wait("sync", ve, ld_gate_ve, future=True)
            P.dma("sync", ych[:], y_tm[c * 128:(c + 1) * 128, half * D:(half + 1) * D], ld)
            P.dma("sync", gch[:], gate_tm[c * 128:(c + 1) * 128, half * D:(half + 1) * D], ld)
            P.dma("sync", cslab[:].rearrange("p (g t) -> p g t", t=128),
                  cvv[:, :, c * 128:(c + 1) * 128].transpose([1, 0, 2]), ld)
            ch_ld = ld.val
            P.wait("vector", ld, ch_ld)
            if (half, c) == (0, 0):
                P.wait("vector", ve, sin_ve)
                P.wait("vector", ve, pref_ve)
            else:
                P.wait("vector", te, ycm_te, future=True)
            vv = P.op("vector", lambda v, c=c: v.tensor_copy(fb_sb[:, 0:1], pref[:, c:c + 1]), ve)
            P.wait("scalar", ve, vv)
            sv = P.op("scalar", lambda s: s.activation(fb_sb[:, 0:1], fb_sb[:, 0:1], AF.Exp), se)
            P.wait("tensor", se, sv)
            tev = P.op("tensor", lambda t: t.transpose(
                ps_r[:], fb_sb[:, 0:1], consts["ident_f"][:]), te)
            P.wait("vector", te, tev)
            vv = P.op("vector", lambda v: v.tensor_copy(colrow[:], ps_r[:]), ve)
            P.wait("tensor", ve, vv)
            tev = P.op("tensor", lambda t: t.matmul(
                ps_m[:], consts["ones_row_f"][0:1, 0:128], colrow[:],
                start=True, stop=True), te)
            P.wait("vector", te, tev)
            vv = P.op("vector", lambda v: v.tensor_copy(fb_sb[:], ps_m[:]), ve)
            P.wait("vector", ve, vv)
            fbv = fb_sb[:].unsqueeze(2).broadcast_to([128, 128, HP])
            sav = s_acc[:].rearrange("p (h q) -> p h q", q=HP)
            scv = scorr[:].rearrange("p (h q) -> p h q", q=HP)
            vv = P.op("vector", lambda v, fbv=fbv, sav=sav, scv=scv: v.tensor_tensor(
                scv, sav, fbv, ALU.mult), ve)
            scorr_ve = vv
            vv = P.op("vector", lambda v, c=c: v.tensor_scalar_mul(
                nch_s[:, 0:128], acs[:, c * 128:(c + 1) * 128], 1.0), ve)
            P.wait("tensor", ve, vv)
            tev = P.op("tensor", lambda t: t.transpose(
                ps_m[:], nch_s[:, 0:128], consts["ident_f"][:]), te)
            P.wait("scalar", te, tev)
            sv = P.op("scalar", lambda s: s.activation(e_ch[:], ps_m[:], AF.Exp), se)
            ech_se = sv
            sv = P.op("scalar", lambda s: s.activation(sgch[:], gch[:], AF.Silu), se)
            sg_se = sv
            P.wait("tensor", ve, scorr_ve)
            if half == 1 and c == 0:
                P.wait("tensor", ve, h0_ev_last)
            for w in range(8):
                wp = w % 2
                h0 = hoff + w * 8
                if (half, c, w) > (0, 0, 1):
                    P.wait("tensor", ve, ycw_prev.get(wp, 0), future=True)
                for hh in range(8):
                    h = h0 + hh
                    gq = h % NGRP
                    tev = P.op("tensor", lambda t, wp=wp, hh=hh, h=h, gq=gq: t.matmul(
                        ps_c[wp][:, hh * 64:(hh + 1) * 64],
                        cslab[:].rearrange("p (g t) -> p g t", t=128)[:, gq, :],
                        scorr[:, h * 64:(h + 1) * 64], start=(hh == 0), stop=(hh == 7)),
                        te if hh == 7 else None)
                P.wait("vector", te, tev)
                P.wait("vector", se, ech_se)
                evw = e_ch[:, h0:h0 + 8]
                evv = evw.unsqueeze(2).broadcast_to([128, 8, 64])
                vv = P.op("vector", lambda v, wp=wp, evv=evv: v.tensor_tensor(
                    yo_w[:].rearrange("p (h q) -> p h q", q=64),
                    ps_c[wp][:].rearrange("p (h q) -> p h q", q=64), evv, ALU.mult), ve)
                P.wait("vector", ve, vv)
                vv = P.op("vector", lambda v, w=w: v.tensor_add(
                    y2[:, w * 512:(w + 1) * 512], yo_w[:], ych[:, w * 512:(w + 1) * 512]), ve)
                if (half, c, w) == (0, 0, 0):
                    ycw_prev = {}
                ycw_prev[wp] = vv
            ycm_te = te.val
            y2_ve = vv
            P.wait("vector", se, sg_se)
            P.wait("vector", ve, y2_ve)
            vv = P.op("vector", lambda v: v.tensor_mul(y2[:], y2[:], sgch[:]), ve)
            xg_ve = vv
            P.wait("scalar", ve, xg_ve)
            for sgi in range(4):
                sv = P.op("scalar", lambda s, sgi=sgi: s.activation(
                    ntm[:, sgi * 1024:(sgi + 1) * 1024],
                    y2[:, sgi * 1024:(sgi + 1) * 1024], AF.Square,
                    accum_out=ms[:, sgi:sgi + 1]), se)
            P.wait("vector", se, sv)
            vv = P.op("vector", lambda v: v.tensor_scalar(
                ms[:], ms[:], 1.0 / 1024.0, 1e-5, ALU.mult, ALU.add), ve)
            P.wait("scalar", ve, vv)
            sv = P.op("scalar", lambda s: s.activation(ms[:], ms[:], AF.Ln))
            sv = P.op("scalar", lambda s: s.activation(ms[:], ms[:], AF.Exp, scale=-0.5), se)
            P.wait("vector", se, sv)
            for sgi in range(4):
                vv = P.op("vector", lambda v, sgi=sgi, half=half: v.scalar_tensor_tensor(
                    ntm[:, sgi * 1024:(sgi + 1) * 1024], y2[:, sgi * 1024:(sgi + 1) * 1024],
                    ms[:, sgi:sgi + 1], nw[:, half * D + sgi * 1024:half * D + (sgi + 1) * 1024],
                    ALU.mult, ALU.mult), ve)
            ntm_ve = vv
            P.wait("tensor", ve, ntm_ve)
            for kc in range(32):
                if not (half == 0 and c == 0 and kc < 2):
                    P.wait("tensor", ve, trv_prev.get(kc % 2, 0), future=True)
                tev = P.op("tensor", lambda t, kc=kc: t.transpose(
                    ps_tr[kc % 2][:], ntm[:, kc * 128:(kc + 1) * 128],
                    consts["ident_b"][:]), te)
                P.wait("vector", te, tev)
                dsl = normed_fm[:].rearrange("p (k t) -> p k t", t=T)[:, kc, c * 128:(c + 1) * 128]
                vv = P.op("vector", lambda v, dsl=dsl, kc=kc: v.tensor_copy(
                    dsl, ps_tr[kc % 2][:]), ve)
                if half == 0 and c == 0 and kc == 0:
                    trv_prev = {}
                trv_prev[kc % 2] = vv
            ld_gate_te = te.val
            ld_gate_ve = xg_ve
            nfm_ve = vv
        if limit == "tail":
            od = getattr(g, "_dbg_nfm", None)
            if od is None:
                od = g.out("dbg_nfm", [128, 32 * T], BF16)
                g._dbg_nfm = od
            if half == 1:
                P.wait("sync", ve, nfm_ve)
                P.dma("sync", od, normed_fm[:], g.st)
                es_t.close()
                es.close()
                return g.st.val
            h0_mm_te = te.val
            h0_last_st = st.val
            h0_ev_last = ve.val
            es_t.close()
            continue
        if getattr(g, "dbg_nfm_full", False):
            od = g.out(f"dbg_nfm{half}", [128, 32 * T], BF16)
            P.wait("sync", ve, nfm_ve)
            P.dma("sync", od, normed_fm[:], g.st)
        es_t.close()
        es_h = ExitStack()
        wsv = woT[half * D:(half + 1) * D, :]
        wsh = WStream(g, f"woh{half}", wsv, D, D, blk=1, es=es_h)
        xres = [g.sb(f"m1c_xr{half}{i}", [128, T], F32, es_h) for i in range(2)]
        xout = [g.sb(f"m1c_xo{half}{i}", [128, T], F32, es_h) for i in range(2)]
        psd = [g.ps(f"m1c_psd{half}{i}", [128, 512], F32, es_h) for i in range(2)]
        te_start = te.val
        P.wait("gpsimd", te, te_start)
        P.wait("gpsimd", ve, nfm_ve)
        P.wait("sync", te, te_start)
        P.wait("sync", ve, nfm_ve)
        consumed = [te_start + (b + 1) for b in range(wsh.nblk)]
        wsh.issue_loads(te, consumed)
        P.wait("tensor", ve, nfm_ve)
        ev_dc = {}
        st_dc = {}
        for dc in range(DC):
            pb = ps_c if dc % 2 == 0 else psd
            if dc >= 2:
                P.wait("sync", ve, ev_dc[dc - 2][0])
            src_res = xsv[dc] if half == 0 else dvv[dc]
            P.dma("sync", xres[dc % 2][:], src_res, ld)
            res_ld = ld.val
            P.wait("tensor", wld, wsh.loaded[dc])
            if dc >= 2:
                P.wait("tensor", ve, ev_dc[dc - 2][1])
            for k in range(DC):
                tile, lv = wsh.tile(dc, k)
                for hf in range(2):
                    last = (k == DC - 1)
                    P.op("tensor", lambda t, tile=tile, hf=hf, last=last, k=k, pb=pb: t.matmul(
                        pb[hf][:], tile,
                        normed_fm[:].rearrange("p (k t) -> p k t", t=T)[:, k, hf * 512:(hf + 1) * 512],
                        start=(k == 0), stop=last), te if (last and hf == 1) else None)
            te_dc = te.val
            P.wait("vector", ld, res_ld)
            P.wait("vector", te, te_dc)
            if dc >= 2:
                P.wait("vector", st, st_dc[dc - 2])
            e0 = P.op("vector", lambda v, dc=dc, pb=pb: v.tensor_add(
                xout[dc % 2][:, 0:512], xres[dc % 2][:, 0:512], pb[0][:]), ve)
            e1 = P.op("vector", lambda v, dc=dc, pb=pb: v.tensor_add(
                xout[dc % 2][:, 512:1024], xres[dc % 2][:, 512:1024], pb[1][:]), ve)
            ev_dc[dc] = (e0, e1)
            P.wait("sync", ve, e1)
            P.dma("sync", dvv[dc], xout[dc % 2][:], st)
            st_dc[dc] = st.val
        h0_mm_te = te.val
        h0_last_st = st.val
        h0_ev_last = ev_dc[DC - 1][1]
        es_h.close()
    es.close()
    return st.val


def host_l1c_weights(inp):
    nwv = np.asarray(inp["L1_attn_norm_weight"]).astype(np.float32)
    return {
        "l1c_woT": _ct(np.asarray(inp["L1_attn_out_proj_w"]).T),
        "l1c_nw": _ct(np.broadcast_to(nwv[None, :], (128, INTER)).copy()),
    }


def host_l1c_routing(core, souts, asums):
    b, j = divmod(core, 4)
    d = {}
    sel = np.zeros((128, 3), dtype=np.float32)
    for i in range(3):
        pc = core - 1 - i
        if pc >= b * 4 and j - 1 - i >= 0:
            d[f"l1c_p{i}S"] = souts[pc]
            d[f"l1c_p{i}A"] = asums[pc]
            sel[:, i] = 1.0
        else:
            d[f"l1c_p{i}S"] = np.zeros((128, INTER), dtype=np.float32)
            d[f"l1c_p{i}A"] = np.zeros((128, 1), dtype=np.float32)
    d["l1c_sel"] = sel
    return d



def kernel(**inputs):
    shared_all = dict(host_shared_consts())
    shared_all.update(host_l0attn_weights(inputs))
    shared_all.update(host_l0ffn_weights(inputs))
    shared_all.update(host_l1a_weights(inputs))
    shared_all.update(host_l1b_weights(inputs))
    shared_all.update(host_l1c_weights(inputs))
    shared_all.update(host_l1ffn_weights(inputs))

    x = np.asarray(inputs["L0_attn_x"], dtype=np.float32)

    g = G("s0")
    consts = make_consts(g)
    xsrc = g.in_("x0T", [D, TH], F32)
    xdst = g.out("x1T", [D, T], F32)
    stage_l0_attn(g, consts, xsrc, xdst)
    finish(g)
    halos = shard_stream_halo(x)
    percore = [dict({"x0T": halos[c]}, **host_l0attn_masks(c)) for c in range(NCORES)]
    res = run_launch(g, percore, shared_all)
    x1 = [r["x1T"] for r in res]

    g = G("s1")
    consts = make_consts(g)
    xsrc = g.in_("x1T", [D, T], F32)
    xdst = g.out("x2T", [D, T], F32)
    stage_l0_ffn(g, consts, xsrc, xdst)
    finish(g)
    percore = [{"x1T": x1[c]} for c in range(NCORES)]
    res = run_launch(g, percore, shared_all)
    x2 = [r["x2T"] for r in res]

    g = G("s2a")
    consts = make_consts(g)
    xsrc = g.in_("x2T", [D, T], F32)
    gate_tm = g.out("gate_tm", [T, INTER], BF16)
    hbcd_f = g.out("hbcd_f", [10368, T], BF16)
    dt32 = g.out("dt32", [128, T], F32)
    stage_l1a(g, consts, xsrc, gate_tm, hbcd_f, dt32)
    finish(g)
    percore = [{"x2T": x2[c]} for c in range(NCORES)]
    res_a = run_launch(g, percore, shared_all)

    g = G("s2b")
    consts = make_consts(g)
    hbcd_in = g.in_("hbcd", [10368, T], BF16)
    halo3 = g.in_("halo3", [10240, 3], BF16)
    dt32i = g.in_("dt32i", [128, T], F32)
    b_dram = g.out("b_f", [1024, T], BF16)
    c_dram = g.out("c_f", [1024, T], BF16)
    y_tm = g.out("y_tm", [T, INTER], BF16)
    s_out = g.out("s_out", [128, INTER], F32)
    a_cs_out = g.out("a_cs", [128, T], F32)
    asum_out = g.out("asum", [128, 1], F32)
    stage_l1b(g, consts, hbcd_in, halo3, dt32i, b_dram, c_dram, y_tm,
              s_out, a_cs_out, asum_out)
    finish(g)
    percore = []
    for core in range(NCORES):
        b, j = divmod(core, 4)
        own = res_a[core]["hbcd_f"]
        if j == 0:
            h3 = np.zeros((10240, 3), dtype=own.dtype)
        else:
            h3 = np.ascontiguousarray(res_a[core - 1]["hbcd_f"][:10240, -3:])
        percore.append({"hbcd": own, "halo3": h3, "dt32i": res_a[core]["dt32"]})
    res_b = run_launch(g, percore, shared_all)

    g = G("s2c")
    consts = make_consts(g)
    xsrc = g.in_("x2T", [D, T], F32)
    y_in = g.in_("y_tm", [T, INTER], BF16)
    c_in = g.in_("c_f", [1024, T], BF16)
    acs_in = g.in_("a_cs_i", [128, T], F32)
    gate_in = g.in_("gate_i", [T, INTER], BF16)
    xdst = g.out("x3T", [D, T], F32)
    stage_l1c(g, consts, xsrc, y_in, c_in, acs_in, gate_in, xdst)
    finish(g)
    souts = [r["s_out"] for r in res_b]
    asums = [r["asum"] for r in res_b]
    percore = []
    for core in range(NCORES):
        d = {"x2T": x2[core], "y_tm": res_b[core]["y_tm"], "c_f": res_b[core]["c_f"],
             "a_cs_i": res_b[core]["a_cs"], "gate_i": res_a[core]["gate_tm"]}
        d.update(host_l1c_routing(core, souts, asums))
        percore.append(d)
    res = run_launch(g, percore, shared_all)
    x3 = [r["x3T"] for r in res]

    g = G("s3")
    consts = make_consts(g)
    xsrc = g.in_("x3T", [D, T], F32)
    xdst = g.out("x4T", [D, T], F32)
    stage_l1_ffn(g, consts, xsrc, xdst)
    finish(g)
    percore = [{"x3T": x3[c]} for c in range(NCORES)]
    res = run_launch(g, percore, shared_all)
    out = unshard_stream([r["x4T"] for r in res])
    return out
